# version 05 (final)
# baseline (speedup 1.0000x reference)
import contextlib
import os as _os
import math
import numpy as np
import ml_dtypes
import concourse.bass as bass
import concourse.mybir as mybir
from concourse.bass_utils import run_bass_kernel_spmd

F32 = mybir.dt.float32
BF16 = mybir.dt.bfloat16
I32 = mybir.dt.int32
AF = mybir.ActivationFunctionType
ALU = mybir.AluOpType
AX = mybir.AxisListType.X
BF = ml_dtypes.bfloat16

NCORE = 8
B = 4
T = 4096
D = 1024
SL = 512
TOK = B * SL
NKC = D // 128
BIGNEG = 30000.0


def cdiv(a, b):
    return -(-a // b)



class Eng:

    def __init__(self, P, name):
        self.P = P
        self.name = name

    def __getattr__(self, meth):
        def call(*args, _inc=None, **kwargs):
            return self.P.op(self.name, meth, args, kwargs, _inc)
        return call


class Prog:
    ENGS = ["tensor", "vector", "scalar", "gpsimd", "sync"]
    DMA_SEM = {"sync": "wy", "gpsimd": "wg", "scalar": "ws", "vector": "wv"}
    ENG_SEM = {"tensor": "mm", "vector": "dv", "scalar": "ev", "gpsimd": "gp",
               "sync": "sy"}

    def __init__(self, name="prog"):
        self._gdbg = None
        self.nc = bass.Bass("TRN2", target_bir_lowering=False, debug=False)
        self.q = {e: [] for e in self.ENGS}
        self.cnt = {}
        self.waited = {}
        self.inputs = {}
        self.input_arrays = {}
        self.outputs = []
        self.T = Eng(self, "tensor")
        self.V = Eng(self, "vector")
        self.S = Eng(self, "scalar")
        self.G = Eng(self, "gpsimd")
        self.Y = Eng(self, "sync")
        self._names = set()
        self._sb_off = 0
        self.lazies = []
        self._wb = None

    def wbufs(self):
        if self._wb is None:
            tiles = [self.sb(f"wbuf{i}", [128, 8 * 512], BF16) for i in range(2)]
            self._wb = {"tiles": tiles, "free": [None, None]}
        return self._wb

    def op(self, eng, meth, args, kwargs, inc):
        semk = None
        if inc is not None:
            sem, k = inc
            self.cnt[sem] = self.cnt.get(sem, 0) + (k if k else 1)
            semk = (sem, k)
            ev = (sem, self.cnt[sem])
        else:
            ev = None
        self.q[eng].append(("op", meth, args, kwargs, semk))
        return ev

    def dma(self, eng, out, in_, sem=None, **kw):
        e = {"sync": self.Y, "gpsimd": self.G, "scalar": self.S}[eng]
        if sem is None:
            sem = self.DMA_SEM[eng]
        return e.dma_start(out=out, in_=in_, _inc=(sem, 16), **kw)

    def wait(self, eng, *evs):
        name = eng.name if isinstance(eng, Eng) else eng
        for e in evs:
            if e is None:
                continue
            sem, val = e
            assert val <= self.cnt.get(sem, 0), (sem, val, self.cnt.get(sem))
            key = (name, sem)
            if self.waited.get(key, 0) >= val:
                continue
            self.q[name].append(("wait", sem, val))
            self.waited[key] = val

    def ev(self, sem):
        return (sem, self.cnt.get(sem, 0))

    def barrier_all(self, evs):
        for e in self.ENGS:
            self.wait(e, *evs)

    def switch_dma(self, arms_fn, n_dma, arm_engine="gpsimd", sem=None):
        if sem is None:
            sem = self.DMA_SEM[arm_engine]
        self.cnt[sem] = self.cnt.get(sem, 0) + 16 * n_dma
        self.q[arm_engine].append(("switch_dma", arms_fn, n_dma, sem))
        return (sem, self.cnt[sem])

    def _uniq(self, name):
        assert name not in self._names, name
        self._names.add(name)
        return name

    def din(self, name, shape, dtype=F32):
        t = self.nc.dram_tensor(self._uniq(name), list(shape), dtype,
                                kind="ExternalInput")
        self.inputs[name] = t
        return t.ap()

    def dout(self, name, shape, dtype=F32):
        t = self.nc.dram_tensor(self._uniq(name), list(shape), dtype,
                                kind="ExternalOutput")
        self.outputs.append(name)
        return t.ap()

    def dram(self, name, shape, dtype=F32, shared=False):
        t = self.nc.dram_tensor(self._uniq(name), list(shape), dtype,
                                kind="Internal",
                                addr_space="Shared" if shared else "Local")
        return t.ap()

    def inp(self, name, arrays, dtype=F32):
        a0 = arrays[0] if isinstance(arrays, list) else arrays
        ap = self.din(name, a0.shape, dtype)
        self.input_arrays[name] = arrays
        return ap

    def sb(self, name, shape, dtype=F32):
        h = _Lazy(self._uniq(name), shape, dtype, "sbuf")
        self.lazies.append(h)
        return h

    def pst(self, name, shape, dtype=F32):
        h = _Lazy(self._uniq(name), shape, dtype, "psum")
        self.lazies.append(h)
        return h

    def finalize(self):
        self.stack = contextlib.ExitStack()
        sems = {}
        all_sems = sorted(self.cnt.keys())
        for s in all_sems:
            sems[s] = self.stack.enter_context(self.nc.semaphore(s))
        for h in self.lazies:
            if h.kind == "sbuf":
                h.t = self.stack.enter_context(
                    self.nc.sbuf_tensor(h.name, list(h.shape), h.dtype))
            else:
                h.t = self.stack.enter_context(
                    self.nc.psum_tensor(h.name, list(h.shape), h.dtype))
        blk = self.stack.enter_context(self.nc.Block())
        P = self

        def replay(e, engname):
            for item in P.q[engname]:
                if item[0] == "op":
                    _, meth, args, kwargs, semk = item
                    args = [a() if isinstance(a, _APThunk) else a for a in args]
                    kwargs = {k: (v() if isinstance(v, _APThunk) else v)
                              for k, v in kwargs.items()}
                    inst = getattr(e, meth)(*args, **kwargs)
                    if semk is not None:
                        sem, k = semk
                        if k is None:
                            inst.then_inc(sems[sem])
                        else:
                            inst.then_inc(sems[sem], k)
                elif item[0] == "wait":
                    _, sem, val = item
                    e.wait_ge(sems[sem], val)
                elif item[0] == "switch_dma":
                    _, arms_fn, n_dma, sem = item
                    for arm in P.nc.switch_core_id(engines=[e], n=128):
                        try:
                            lid = arm.logical % 8
                        except Exception:
                            lid = 0
                        descs = arms_fn(lid)
                        nd = 0
                        for dsc in descs:
                            if dsc[0] == "memset":
                                e.memset(dsc[1]() if isinstance(dsc[1], _APThunk)
                                         else dsc[1], dsc[2])
                            else:
                                out_ap, in_ap = dsc[0], dsc[1]
                                kw = dsc[2] if len(dsc) > 2 else {}
                                out_ap = out_ap() if isinstance(out_ap, _APThunk) else out_ap
                                in_ap = in_ap() if isinstance(in_ap, _APThunk) else in_ap
                                e.dma_start(out=out_ap, in_=in_ap,
                                            **kw).then_inc(sems[sem], 16)
                                nd += 1
                        assert nd == n_dma, (engname, lid, nd, n_dma)

        for engname in self.ENGS:
            if not self.q[engname]:
                continue
            deco = getattr(blk, {"tensor": "tensor", "vector": "vector",
                                 "scalar": "scalar", "gpsimd": "gpsimd",
                                 "sync": "sync"}[engname])

            def body(e, engname=engname):
                replay(e, engname)
            deco(body)
        self.stack.close()
        return self.nc


class _Lazy:

    def __init__(self, name, shape, dtype, kind):
        self.name = name
        self.shape = shape
        self.dtype = dtype
        self.kind = kind
        self.t = None

    def __getitem__(self, idx):
        return _APThunk(self, ("getitem", idx))

    def ap(self):
        return _APThunk(self, ("ap",))


class _APThunk:
    def __init__(self, src, opdesc, parent=None):
        self.src = src
        self.opdesc = opdesc
        self.parent = parent

    def __call__(self):
        if self.parent is not None:
            base = self.parent()
        else:
            base = self.src.t
        kind = self.opdesc[0]
        if kind == "getitem":
            return base[self.opdesc[1]]
        if kind == "ap":
            return base.ap() if hasattr(base, "ap") else base
        if kind == "meth":
            _, name, args, kwargs = self.opdesc
            return getattr(base, name)(*args, **kwargs)
        raise ValueError(kind)

    def __getattr__(self, name):
        def call(*args, **kwargs):
            return _APThunk(self.src, ("meth", name, args, kwargs), parent=self)
        return call

    def __getitem__(self, idx):
        return _APThunk(self.src, ("getitem", idx), parent=self)





def alibi_slopes(n):
    def pow2(m):
        start = 2 ** (-2 ** (-(math.log2(m) - 3)))
        return [start * start ** i for i in range(m)]
    if math.log2(n).is_integer():
        return pow2(n)
    c = 2 ** math.floor(math.log2(n))
    return pow2(c) + alibi_slopes(2 * c)[0::2][:n - c]


def rotary_tables():
    DIM_HEAD, WINDOW = 64, 512
    half = np.arange(0, DIM_HEAD, 2, dtype=np.float64)
    inv_freq = 1.0 / 10000.0 ** (half / DIM_HEAD)
    t = np.arange(T, dtype=np.float64)
    freqs = t[:, None] * inv_freq[None, :]
    freqs = np.concatenate([freqs, freqs], axis=-1)
    base = (half + 0.4 * DIM_HEAD) / (1.4 * DIM_HEAD)
    power = (t - T // 2) / float(WINDOW // 2)
    scale = base[None, :] ** power[:, None]
    scale = np.concatenate([scale, scale], axis=-1)
    cos, sin = np.cos(freqs), np.sin(freqs)
    return ((cos * scale).T.astype(np.float32), (sin * scale).T.astype(np.float32),
            (cos / scale).T.astype(np.float32), (sin / scale).T.astype(np.float32))



class WeightStream:

    def __init__(self, P, dram_ap, kin, mout, mgw=512, name="w", bufs=None):
        self.P = P
        self.d = dram_ap
        self.kin = kin
        self.mout = mout
        self.nk = cdiv(kin, 128)
        mgw = min(mgw, max(128, (8 * 512 // self.nk) // 128 * 128))
        self.mgw = mgw
        self.ngrp = cdiv(mout, mgw)
        assert self.nk * mgw <= 8 * 512, (kin, mgw)
        self.wb = P.wbufs()
        self.bufs = self.wb["tiles"]

    def load(self, gi):
        P = self.P
        bi = gi % 2
        buf = self.bufs[bi]
        m0 = gi * self.mgw
        mw = min(self.mgw, self.mout - m0)
        if self.wb["free"][bi] is not None:
            P.wait(P.G, self.wb["free"][bi])
        ev = None
        for k in range(self.nk):
            kw = min(128, self.kin - 128 * k)
            ev = P.dma("gpsimd", buf[0:kw, k * self.mgw:k * self.mgw + mw],
                       self.d[128 * k:128 * k + kw, m0:m0 + mw],
                       sem=f"wgt{bi}")
        return buf, ev, mw

    def set_free(self, gi, ev):
        self.wb["free"][gi % 2] = ev


class Psum:

    def __init__(self, P):
        self.P = P
        self.arena = P.pst("psum_arena", [128, 4096], F32)
        self.free = list(range(8))
        self.guard = [[] for _ in range(8)]

    def alloc(self, nbanks, eng=None):
        assert len(self.free) >= nbanks, "psum arena exhausted"
        self.free.sort()
        run = None
        for s in range(len(self.free)):
            b0 = self.free[s]
            if all((b0 + j) in self.free for j in range(nbanks)):
                run = b0
                break
        assert run is not None, "psum fragmentation"
        banks = tuple(range(run, run + nbanks))
        for b in banks:
            self.free.remove(b)
        evs = [e for b in banks for e in self.guard[b]]
        if eng is not None:
            for e in evs:
                self.P.wait(eng, e)
            evs = []
        ap = self.arena[:, 512 * run:512 * (run + nbanks)]
        return banks, ap, evs

    def release(self, banks, ev):
        evl = ev if isinstance(ev, list) else ([ev] if ev is not None else [])
        for b in banks:
            self.free.append(b)
            self.guard[b] = list(evl)


class PsumPool:

    def __init__(self, P, psum, n, nbanks=1):
        self.P = P
        self.psum = psum
        self.items = []
        for i in range(n):
            banks, ap, evs = psum.alloc(nbanks)
            self.items.append([banks, ap, list(evs)])
        self.i = 0
        self.n = n

    def acquire(self, eng):
        it = self.items[self.i % self.n]
        self.i += 1
        for e in it[2]:
            self.P.wait(eng, e)
        it[2] = []
        return it[1], self.i - 1

    def release(self, idx, ev):
        self.items[idx % self.n][2] = [ev]

    def free_all(self):
        for banks, ap, ev in self.items:
            self.psum.release(banks, ev)


def e_matmul_stack(P, ws, x_src, ntok, out_cb, psum_pool, nbw=512,
                   extra_mm=None):
    nnb = cdiv(ntok, nbw)
    pending = {0: ws.load(0)}
    for gi in range(ws.ngrp):
        buf, lev, mw_g = pending.pop(gi)
        if gi + 1 < ws.ngrp:
            pending[gi + 1] = ws.load(gi + 1)
        m0g = gi * ws.mgw
        last_cons = None
        for mb in range(cdiv(mw_g, 128)):
            m0 = m0g + 128 * mb
            mw = min(128, ws.mout - m0)
            for nb in range(nnb):
                n0 = nb * nbw
                nw = min(nbw, ntok - n0)
                pstile, pi = psum_pool.acquire(P.T)
                P.wait(P.T, lev)
                for k in range(ws.nk):
                    kw = min(128, ws.kin - 128 * k)
                    xap, xev = x_src(k, n0, nw)
                    if xev is not None:
                        P.wait(P.T, xev)
                    lhs = buf[0:kw, k * ws.mgw + 128 * mb:k * ws.mgw + 128 * mb + mw]
                    stop = (k == ws.nk - 1) and extra_mm is None
                    mm_ev = P.T.matmul(pstile[0:mw, 0:nw], lhs, xap,
                                       start=(k == 0), stop=stop,
                                       _inc=("mm", 1) if stop else None)
                if extra_mm is not None:
                    mm_ev = extra_mm(m0, mw, n0, nw, pstile)
                cons = out_cb(m0, mw, n0, nw, pstile, mm_ev)
                psum_pool.release(pi, cons)
                last_cons = cons
        ws.set_free(gi, last_cons)


class NormScratch:

    def __init__(self, P, psum, name="lnsp"):
        self.psum = psum
        self.ones_row = P.sb(name + "_onesr", [1, 128], F32)
        self.ones_col = P.sb(name + "_onesc", [128, 1], F32)
        self.ones_row_b = P.sb(name + "_onesrb", [1, 128], BF16)
        self.ones_col_b = P.sb(name + "_onescb", [128, 1], BF16)
        P.G.memset(self.ones_row[:, :], 1.0)
        P.G.memset(self.ones_col[:, :], 1.0)
        P.G.memset(self.ones_row_b[:, :], 1.0)
        ev = P.G.memset(self.ones_col_b[:, :], 1.0, _inc=("gp", 1))
        P.wait(P.T, ev)
        self.sq = P.sb(name + "_sq", [128, 512], F32)
        self.sqb = P.sb(name + "_sqb", [128, 512], BF16)
        self.mu = P.sb(name + "_mu", [1, 512], F32)
        self.rstd = P.sb(name + "_rstd", [1, 512], F32)
        self.mrs = P.sb(name + "_mrs", [1, 512], F32)


def e_layernorm(P, sp, x_ap_fn, ntok, write_out, eps=1e-5,
                mode="ln", nchunks=NKC, denom=None, bf_in=False):
    dn = denom or (nchunks * 128)
    last = None
    stb, stap, _ = sp.psum.alloc(1, P.T)
    b1b, b1ap, _ = sp.psum.alloc(1, P.T)
    b2b, b2ap, _ = sp.psum.alloc(1, P.T)
    for nb in range(cdiv(ntok, 512)):
        n0 = nb * 512
        nw = min(512, ntok - n0)
        mm_ev = None
        for k in range(nchunks):
            xap, x_ev = x_ap_fn(k, n0, nw)
            if x_ev is not None:
                P.wait(P.S, x_ev)
                P.wait(P.T, x_ev)
            if k == 0 and last is not None:
                P.wait(P.S, last)
            sqt = sp.sqb if bf_in else sp.sq
            onc = sp.ones_col_b if bf_in else sp.ones_col
            sqe = P.S.activation(out=sqt[:, 0:nw], in_=xap, func=AF.Square,
                                 _inc=("ev", 1))
            P.wait(P.T, sqe)
            if mode == "ln":
                P.T.matmul(stap[0:1, 0:nw], onc[:, :], xap,
                           start=(k == 0), stop=(k == nchunks - 1))
            mm_ev = P.T.matmul(stap[32:33, 0:nw], onc[:, :],
                               sqt[:, 0:nw], start=(k == 0),
                               stop=(k == nchunks - 1),
                               _inc=("mm", 1))
            P.wait(P.S, mm_ev)
        P.wait(P.S, mm_ev)
        if mode == "ln":
            P.S.mul(out=sp.mu[0:1, 0:nw], in_=stap[0:1, 0:nw], mul=1.0 / dn)
        ev2 = P.S.mul(out=sp.rstd[0:1, 0:nw], in_=stap[32:33, 0:nw],
                      mul=1.0 / dn, _inc=("ev", 1))
        P.wait(P.V, ev2)
        if mode == "ln":
            P.V.tensor_tensor(out=sp.mrs[0:1, 0:nw], in0=sp.mu[0:1, 0:nw],
                              in1=sp.mu[0:1, 0:nw], op=ALU.mult)
            ev3 = P.V.tensor_tensor(out=sp.rstd[0:1, 0:nw], in0=sp.rstd[0:1, 0:nw],
                                    in1=sp.mrs[0:1, 0:nw], op=ALU.subtract,
                                    _inc=("dv", 1))
            P.wait(P.S, ev3)
        P.wait(P.V, P.ev("ev"))
        ev3b = P.V.tensor_scalar(out=sp.rstd[0:1, 0:nw], in0=sp.rstd[0:1, 0:nw],
                                 scalar1=float(eps), scalar2=None, op0=ALU.add,
                                 _inc=("dv", 1))
        P.wait(P.S, ev3b)
        ev4 = P.S.activation(out=sp.rstd[0:1, 0:nw], in_=sp.rstd[0:1, 0:nw],
                             func=AF.Sqrt, _inc=("ev", 1))
        P.wait(P.V, ev4)
        ev5 = P.V.reciprocal(out=sp.rstd[0:1, 0:nw], in_=sp.rstd[0:1, 0:nw],
                             _inc=("dv", 1))
        if mode == "ln":
            ev5 = P.V.tensor_tensor(out=sp.mrs[0:1, 0:nw], in0=sp.mu[0:1, 0:nw],
                                    in1=sp.rstd[0:1, 0:nw], op=ALU.mult,
                                    _inc=("dv", 1))
        P.wait(P.T, ev5)
        bc_ev = P.T.matmul(b1ap[:, 0:nw], sp.ones_row[:, :], sp.rstd[0:1, 0:nw],
                           start=True, stop=True, _inc=("mm", 1))
        if mode == "ln":
            bc_ev = P.T.matmul(b2ap[:, 0:nw], sp.ones_row[:, :], sp.mrs[0:1, 0:nw],
                               start=True, stop=True, _inc=("mm", 1))
        P.wait(P.V, bc_ev)
        for k in range(nchunks):
            xap, x_ev = x_ap_fn(k, n0, nw)
            if x_ev is not None:
                P.wait(P.V, x_ev)
            P.V.tensor_tensor(out=sp.sq[:, 0:nw], in0=xap, in1=b1ap[:, 0:nw],
                              op=ALU.mult)
            if mode == "ln":
                P.V.tensor_tensor(out=sp.sq[:, 0:nw], in0=sp.sq[:, 0:nw],
                                  in1=b2ap[:, 0:nw], op=ALU.subtract)
            last = write_out(k, n0, nw, sp.sq[:, 0:nw])
    sp.psum.release(stb, last)
    sp.psum.release(b1b, last)
    sp.psum.release(b2b, last)
    return last



def e_proj_fm(P, w_dram, xn_bf, kin, mout, ntok, out_cb, psum, npool=3,
              name="p", mgw=512, wbufs=None, extra_mm=None, x_ev=None):
    ws = WeightStream(P, w_dram, kin, mout, mgw=mgw, name=name, bufs=wbufs)
    pool = PsumPool(P, psum, npool)

    if callable(xn_bf):
        def x_src(k, n0, nw):
            return xn_bf(k, n0, nw), x_ev
    else:
        def x_src(k, n0, nw):
            return xn_bf[:, k * ntok + n0:k * ntok + n0 + nw], x_ev

    e_matmul_stack(P, ws, x_src, ntok, out_cb, pool, extra_mm=extra_mm)
    pool.free_all()
    return ws


def e_proj_tok(P, w_dram, xn_bf, kin, mout, ntok, out_cb, psum, npool=3,
               name="pt", nbw=512, extra=None, x_ev=None):
    nk = cdiv(kin, 128)
    ws = WeightStream(P, w_dram, kin, mout, mgw=nbw, name=name)
    pool = PsumPool(P, psum, npool)
    pending = {0: ws.load(0)}
    for gi in range(ws.ngrp):
        buf, lev, mw_g = pending.pop(gi)
        if gi + 1 < ws.ngrp:
            pending[gi + 1] = ws.load(gi + 1)
        m0 = gi * nbw
        last = None
        for ti in range(cdiv(ntok, 128)):
            pstile, pi = pool.acquire(P.T)
            P.wait(P.T, lev)
            if x_ev is not None:
                P.wait(P.T, x_ev)
            for k in range(nk):
                kw = min(128, kin - 128 * k)
                lhs = xn_bf[:, k * ntok + 128 * ti:k * ntok + 128 * ti + 128]
                laststep = (k == nk - 1) and extra is None
                mm_ev = P.T.matmul(pstile[0:128, 0:mw_g], lhs,
                                   buf[0:kw, k * nbw:k * nbw + mw_g],
                                   start=(k == 0), stop=laststep,
                                   _inc=("mm", 1) if laststep else None)
            if extra is not None:
                ones_ap, row_tile = extra
                mm_ev = P.T.matmul(pstile[0:128, 0:mw_g], ones_ap,
                                   row_tile[0:1, m0:m0 + mw_g],
                                   start=False, stop=True, _inc=("mm", 1))
            cons = out_cb(ti, m0, mw_g, pstile, mm_ev)
            pool.release(pi, cons)
            last = cons
        ws.set_free(gi, last)
    pool.free_all()


def e_gather(P, contrib_dram, gathered_dram, ready_evs, name):
    P.wait(P.G, *ready_evs)
    ev = P.G.collective_compute(
        "AllGather", ALU.bypass,
        replica_groups=[list(range(NCORE))],
        ins=[contrib_dram.opt()], outs=[gathered_dram.opt()],
        _inc=("cc", None))
    return ev


class DramRMW:

    _inst = [0]

    def __init__(self, P, name, dram_ap, n=2):
        self.P = P
        self.d = dram_ap
        self.tiles = [P.sb(f"{name}_rmw{i}", [128, 512], F32) for i in range(n)]
        self.guard = [None] * n
        self.i = 0
        DramRMW._inst[0] += 1
        self.sems = [f"rmw{DramRMW._inst[0]}_{i}" for i in range(n)]

    def add_psum(self, mb, n0, nw, src_ap, src_ev, ntok=TOK):
        P = self.P
        i = self.i % len(self.tiles)
        self.i += 1
        if self.guard[i] is not None:
            P.wait(P.Y, self.guard[i])
        ld = P.dma("sync", self.tiles[i][:, 0:nw],
                   self.d[128 * mb:128 * mb + 128, n0:n0 + nw],
                   sem=self.sems[i])
        P.wait(P.V, ld, src_ev)
        ad = P.V.tensor_tensor(out=self.tiles[i][:, 0:nw], in0=self.tiles[i][:, 0:nw],
                               in1=src_ap, op=ALU.add, _inc=("dv", 1))
        P.wait(P.Y, ad)
        st = P.dma("sync", self.d[128 * mb:128 * mb + 128, n0:n0 + nw],
                   self.tiles[i][:, 0:nw], sem=self.sems[i])
        self.guard[i] = st
        return st


class DramSrc:

    _inst = [0]

    def __init__(self, P, name, dram_ap, n=3, ntok=TOK):
        self.P = P
        self.d = dram_ap
        self.ntok = ntok
        self.tiles = [P.sb(f"{name}_src{i}", [128, 512], F32) for i in range(n)]
        self.used = [False] * n
        self.i = 0
        DramSrc._inst[0] += 1
        self.sems = [f"dsrc{DramSrc._inst[0]}_{i}" for i in range(n)]

    def get(self, k, n0, nw):
        P = self.P
        i = self.i % len(self.tiles)
        self.i += 1
        if self.used[i]:
            P.wait(P.Y, P.ev("dv"))
            P.wait(P.Y, P.ev("mm"))
            P.wait(P.Y, P.ev("ev"))
        self.used[i] = True
        ld = P.dma("sync", self.tiles[i][:, 0:nw],
                   self.d[128 * k:128 * k + 128, n0:n0 + nw],
                   sem=self.sems[i])
        return self.tiles[i][:, 0:nw], ld



class AttnCtx:

    def __init__(self, P, psum, lname):
        self.psum = psum
        n = lname
        self.ind_a = P.sb(n + "_inda", [128, 1], BF16)
        self.ind_b = P.sb(n + "_indb", [128, 1], BF16)
        P.G.memset(self.ind_a[:, :], 0.0)
        P.G.memset(self.ind_a[0:64, :], 1.0)
        P.G.memset(self.ind_b[:, :], 0.0)
        ev = P.G.memset(self.ind_b[64:128, :], 1.0, _inc=("gp", 1))
        self.ones2k = P.sb(n + "_ones2k", [1, TOK], BF16)
        ev = P.G.memset(self.ones2k[:, :], 1.0, _inc=("gp", 1))
        self.onesr64 = P.sb(n + "_onesr64", [1, 64], F32)
        ev = P.G.memset(self.onesr64[:, :], 1.0, _inc=("gp", 1))
        self.eye_b = P.sb(n + "_eyeb", [128, 128], BF16)
        eye_d = P.inp(n + "_eyed", np.eye(128, dtype=np.float32), F32)
        ev = P.dma("gpsimd", self.eye_b[:, :], eye_d[:, :], sem="cstg")
        self.init_ev = ev


def e_attn_big(P, cfg, xn_bf, xr, psum, ctx, phase="full", dbg=None):
    nm = cfg["name"]
    NH = 16
    qbuf = P.sb(nm + "_qbuf", [128, NKC * TOK], BF16)
    kstage = P.sb(nm + "_kst", [128, 512], BF16)
    sqstage = P.sb(nm + "_sqst", [128, 512], BF16)
    knstage = P.sb(nm + "_knst", [1, 512], BF16)
    sqq = P.sb(nm + "_sqq", [64, TOK], BF16)
    qn2h = P.sb(nm + "_qn2h", [1, TOK], BF16)
    fst = P.sb(nm + "_fst", [128, 512], F32)
    arena = P.sb(nm + "_arena", [128, 16384 + 4 * 2080], BF16)
    vcontrib = arena[:, 0:16640]
    kunits = [arena[0:66, b * T:(b + 1) * T] for b in range(B)]
    vunits = [arena[:, 16384 + b * 2080:16384 + (b + 1) * 2080] for b in range(B)]
    knrow = P.sb(nm + "_knrow", [1, T], BF16)
    km2 = P.sb(nm + "_km2", [1, B], F32)
    mrow = P.sb(nm + "_mrow", [1, 512], F32)
    qunit = P.sb(nm + "_qunit", [66, TOK], BF16)
    pt = [P.sb(f"{nm}_pt{i}", [128, TOK], BF16) for i in range(2)]
    ostage = P.sb(nm + "_ost", [64, 512], BF16)
    rrow = P.sb(nm + "_rrow", [1, 512], F32)

    kcon = P.dram(nm + "_kcon", [NH * 65 + NH, TOK], BF16)
    vcon = P.dram(nm + "_vcon", [128, 16 * 1040], BF16)
    kg = P.dram(nm + "_kg", [NCORE, NH * 65 + NH, TOK], BF16, shared=True)
    vg = P.dram(nm + "_vg", [NCORE, 128, 16 * 1040], BF16, shared=True)
    kgr = P.dram(nm + "_kgr", [NH * 65 + 1, B * T], BF16)
    vgr = P.dram(nm + "_vgr", [128, B * NH * 32 * 65], BF16)

    flagr = P.inp(nm + "_flag", cfg["flag_rot"], BF16)
    if cfg.get("dve_add") is not None:
        arrs = cfg["dve_add"]
        a0 = arrs[0] if isinstance(arrs, list) else arrs
        nD = a0.shape[0]
        dveD = P.inp(nm + "_dved", arrs, BF16)
        dvet = [P.sb(f"{nm}_dvet{i}", [128, 512], BF16) for i in range(2)]
    else:
        dveD, nD = None, 0
    if cfg.get("expbias") is not None:
        ebias = P.inp(nm + "_ebias", cfg["expbias"], F32)
        ebt = P.sb(nm + "_ebt", [128, NH * 32], F32)
        eb_ev = P.dma("sync", ebt[:, :], ebias[:, :], sem="cst")
    else:
        ebt, eb_ev = None, None

    qb_col, kb_col, ob_col = cfg.get("qb_col"), cfg.get("kb_col"), cfg.get("ob_col")
    vb_row = cfg.get("vb_row")

    ev0 = P.G.memset(vcontrib, 0.0)
    ev_ones = P.G.memset(
        vcontrib.rearrange("p (m s) -> p m s", s=65)[:, :, 64:65], 1.0,
        _inc=("gp", 1))
    mstageb = P.sb(nm + "_mstageb", [1, 2 * TOK], BF16)
    qu_ms = P.G.nop(_inc=("gp", 1))

    def q_cb(m0, mw, n0, nw, pstile, ev_mm):
        mb = m0 // 128
        out_ap = qbuf[0:mw, mb * TOK + n0:mb * TOK + n0 + nw]
        if qb_col is not None:
            P.wait(P.V, ev_mm)
            return P.V.tensor_scalar(out=out_ap, in0=pstile[0:mw, 0:nw],
                                     scalar1=0.125, scalar2=qb_col(mb),
                                     op0=ALU.mult, op1=ALU.add, _inc=("dv", 1))
        P.wait(P.S, ev_mm)
        return P.S.activation(out=out_ap, in_=pstile[0:mw, 0:nw],
                              func=AF.Copy, scale=0.125, _inc=("ev", 1))

    if _os.environ.get("QSKIP") != "1":
        e_proj_fm(P, cfg["wq"], xn_bf, D, D, TOK, q_cb, psum, name=nm + "q")
    if phase == "qproj":
        P.wait(P.G, P.ev("dv"), P.ev("ev"))
        for k in range(NKC):
            P.dma("gpsimd", xr[128 * k:128 * (k + 1), :], qbuf[:, k * TOK:(k + 1) * TOK])
        return P.ev("wg")

    kn_b, kn_ap, _ = psum.alloc(1, P.T)
    kdma = []
    kcb_state = {"prev": None, "prev_d": None}

    def k_cb(m0, mw, n0, nw, pstile, ev_mm):
        mb = m0 // 128
        if _os.environ.get("KDBG") == "min":
            P.wait(P.S, ev_mm)
            e1 = P.S.activation(out=kstage[:, 0:nw], in_=pstile[0:mw, 0:nw],
                                func=AF.Copy, _inc=("ev", 1))
            P.wait(P.Y, e1)
            d2 = P.dma("sync", kcon[0:128, n0:n0 + nw], kstage[:, 0:nw])
            kdma.append(d2)
            P.wait(P.S, d2)
            return e1
        if kb_col is not None:
            P.wait(P.V, ev_mm)
            e1 = P.V.tensor_scalar(out=kstage[:, 0:nw], in0=pstile[0:mw, 0:nw],
                                   scalar1=kb_col(mb), scalar2=None,
                                   op0=ALU.add, _inc=("dv", 1))
            P.wait(P.S, e1)
        else:
            P.wait(P.S, ev_mm)
            e1 = P.S.activation(out=kstage[:, 0:nw], in_=pstile[0:mw, 0:nw],
                                func=AF.Copy, _inc=("ev", 1))
        e2 = P.S.activation(out=sqstage[:, 0:nw], in_=kstage[:, 0:nw],
                            func=AF.Square, _inc=("ev", 1))
        h0, h1 = 2 * mb, 2 * mb + 1
        P.wait(P.Y, e1)
        if _os.environ.get("KDBG") != "nodma":
            P.dma("sync", kcon[65 * h0:65 * h0 + 64, n0:n0 + nw], kstage[0:64, 0:nw])
            d2 = P.dma("sync", kcon[65 * h1:65 * h1 + 64, n0:n0 + nw],
                       kstage[64:128, 0:nw])
        elif _os.environ.get("KDBG") == "nodma":
            P.dma("sync", kcon[0:64, n0:n0 + nw], kstage[0:64, 0:nw])
            d2 = P.dma("sync", kcon[64:128, n0:n0 + nw], kstage[64:128, 0:nw])
        else:
            d2 = P.dma("sync", kcon[0:128, n0:n0 + nw], kstage[:, 0:nw])
        if _os.environ.get("KDBG") == "nonorm":
            kdma.append(d2)
            P.wait(P.S, d2)
            return e1
        P.wait(P.T, e2, ctx.init_ev)
        if kcb_state["prev"] is not None:
            P.wait(P.T, kcb_state["prev"])
        P.T.matmul(kn_ap[0:1, 0:nw], ctx.ind_a[:, :], sqstage[:, 0:nw],
                   start=True, stop=True)
        m2 = P.T.matmul(kn_ap[32:33, 0:nw], ctx.ind_b[:, :], sqstage[:, 0:nw],
                        start=True, stop=True, _inc=("mm", 1))
        P.wait(P.S, m2)
        e3 = P.S.copy(out=knstage[0:1, 0:nw], in_=kn_ap[0:1, 0:nw], _inc=("ev", 1))
        P.wait(P.Y, e3)
        d3_ = P.dma("sync", kcon[16 * 65 + h0:16 * 65 + h0 + 1, n0:n0 + nw],
                    knstage[0:1, 0:nw], sem="akc")
        P.wait(P.S, d3_)
        e4 = P.S.copy(out=knstage[0:1, 0:nw], in_=kn_ap[32:33, 0:nw], _inc=("ev", 1))
        P.wait(P.Y, e4)
        d4 = P.dma("sync", kcon[16 * 65 + h1:16 * 65 + h1 + 1, n0:n0 + nw],
                   knstage[0:1, 0:nw], sem="akc")
        kdma.append(d4)
        kcb_state["prev"] = e4
        kcb_state["prev_d"] = d4
        return e1

    if phase != "vcon":
        e_proj_fm(P, cfg["wk"], xn_bf, D, D, TOK, k_cb, psum, name=nm + "k")
    if phase == "kcon":
        P.wait(P.G, *kdma)
        return P.ev("wy")
    P.wait(P.Y, ctx.init_ev)
    for h in range(NH):
        dko = P.dma("sync", kcon[65 * h + 64:65 * h + 65, :], ctx.ones2k[:, :],
                    sem="akc")
    kdma.append(dko)
    psum.release(kn_b, P.ev("ev"))

    def v_cb(ti, m0, mw, pstile, ev_mm):
        b_, jl = ti // 4, ti % 4
        g0 = m0 // 64
        nheads = mw // 64
        P.wait(P.S, ev_mm, ev0, ev_ones)
        mstart = (b_ * 16 + g0) * 4 + jl
        sel = vcontrib.rearrange("p (m s) -> p m s", s=65)[
            :, mstart:mstart + 4 * (nheads - 1) + 1:4, 0:64]
        return P.S.activation(out=sel,
                              in_=pstile[0:128, 0:mw].rearrange(
                                  "p (h s) -> p h s", s=64),
                              func=AF.Copy, _inc=("ev", 1))

    e_proj_tok(P, cfg["wv"], xn_bf, D, D, TOK, v_cb, psum, name=nm + "v",
               extra=cfg.get("v_extra"))

    P.wait(P.Y, P.ev("ev"))
    vdma = [P.dma("sync", vcon[:, :], vcontrib, sem="avc")]

    if phase in ("kvcon", "vcon"):
        P.wait(P.G, *(kdma if phase == "kvcon" else []), *vdma)
        return P.ev("wy")

    cc1 = e_gather(P, kcon, kg, kdma, nm + "gk")
    cc2 = e_gather(P, vcon, vg, vdma, nm + "gv")

    vtmp = P.dram(nm + "_vtmp", [128, B * NH * 32 * 65], BF16)
    P.wait(P.G, cc2)
    for b_ in range(B):
        for h_ in range(NH):
            g0 = (b_ * NH + h_)
            hv_ev = P.dma("gpsimd",
                          vtmp[:, g0 * 2080:(g0 + 1) * 2080]
                          .rearrange("p (r js) -> p r js", r=NCORE),
                          vg[:, :, g0 * 260:(g0 + 1) * 260]
                          .rearrange("r p js -> p r js"), sem="rear")

    def rearr_arms(lid):
        descs = []
        rng = [(lid, 8), (0, lid)] if lid > 0 else [(0, 7), (7, 8)]
        for b_ in range(B):
            cc_off = 0
            for (r0, r1) in rng:
                nr = r1 - r0
                descs.append((
                    kgr[0:16 * 65, b_ * T + cc_off * 512:b_ * T + (cc_off + nr) * 512]
                    .rearrange("q (r c) -> q r c", r=nr),
                    kg[r0:r1, 0:16 * 65, b_ * SL:(b_ + 1) * SL]
                    .rearrange("r q c -> q r c"), {}))
                cc_off += nr
        sh = (4 * lid % 32) * 65
        if sh == 0:
            descs.append((
                vgr[:, :].rearrange("p (g c) -> p g c", g=B * NH),
                vtmp[:, :].rearrange("p (g c) -> p g c", g=B * NH), {}))
            descs.append((
                vgr[:, 0:65].rearrange("p (g c) -> p g c", g=1),
                vtmp[:, 0:65].rearrange("p (g c) -> p g c", g=1), {}))
        else:
            descs.append((
                vgr[:, :].rearrange("p (g c) -> p g c", g=B * NH)[:, :, 0:2080 - sh],
                vtmp[:, :].rearrange("p (g c) -> p g c", g=B * NH)[:, :, sh:2080], {}))
            descs.append((
                vgr[:, :].rearrange("p (g c) -> p g c", g=B * NH)[:, :, 2080 - sh:2080],
                vtmp[:, :].rearrange("p (g c) -> p g c", g=B * NH)[:, :, 0:sh], {}))
        return descs

    P.wait(P.G, cc1, cc2, hv_ev)
    rear_ev = P.switch_dma(rearr_arms, n_dma=B * 2 + 2, arm_engine="gpsimd")
    fl_ev = P.dma("sync", kgr[16 * 65:16 * 65 + 1, 0:T], flagr[:, :])
    if phase == "gather":
        P.wait(P.G, rear_ev, fl_ev)
        return P.ev("wg")

    if phase == "units":
        dbk = P.dout(nm + "_dbgk", [66, T], BF16)
        dbv = P.dout(nm + "_dbgv", [128, 32 * 65], BF16)
        dbn = P.dout(nm + "_dbgn", [1, T], BF16)
        h, b_ = cfg.get("dbg_h", 0), cfg.get("dbg_b", 0)
        P.wait(P.Y, rear_ev, cc1, cc2, fl_ev)
        e1 = P.dma("sync", kunits[b_][0:65, :],
                   kgr[65 * h:65 * h + 65, b_ * T:(b_ + 1) * T])
        e1b = P.dma("sync", kunits[b_][65:66, :], kgr[16 * 65:16 * 65 + 1, 0:T])
        e2 = P.dma("sync", vunits[b_][:, :],
                   vgr[:, :].rearrange("p (bb h c) -> p bb h c",
                                       bb=B, h=16, c=32 * 65)[:, b_, h, :])
        e3 = P.dma("sync", knrow[0:1, :].rearrange("q (r c) -> q r c", r=8),
                   kg[:, 16 * 65 + h:16 * 65 + h + 1, b_ * SL:(b_ + 1) * SL]
                   .rearrange("r q c -> q r c"))
        P.wait(P.Y, e1, e1b, e2, e3)
        P.dma("sync", dbk[:, :], kunits[b_][:, :])
        P.dma("sync", dbv[:, :], vunits[b_][:, :])
        e9 = P.dma("sync", dbn[:, :], knrow[0:1, :])
        P.wait(P.G, e9)
        return P.ev("wy")

    stb4, st4ap, _ = psum.alloc(4, P.T)
    obs = [psum.alloc(1, P.T) for _ in range(B)]
    bcap = st4ap
    pv_hist = []
    ow_ev = None
    dv_hist = []
    for h in range(NH):
        ku_evs = []
        P.wait(P.Y, rear_ev, cc1, cc2, fl_ev)
        if h > 0:
            P.wait(P.Y, P.ev("mm"))
            P.wait(P.G, P.ev("mm"))
            P.wait(P.S, P.ev("mm"))
        for b_ in range(B):
            e1 = P.dma("sync", kunits[b_][0:65, :],
                       kgr[65 * h:65 * h + 65, b_ * T:(b_ + 1) * T], sem="aku")
            e1b = P.dma("sync", kunits[b_][65:66, :],
                        kgr[16 * 65:16 * 65 + 1, 0:T], sem="aku")
            e2 = P.dma("sync", vunits[b_][:, :],
                       vgr[:, :].rearrange("p (bb h c) -> p bb h c",
                                           bb=B, h=16, c=32 * 65)[:, b_, h, :],
                       sem="aku")
            e3 = P.dma("sync", knrow[0:1, :].rearrange("q (r c) -> q r c", r=8),
                       kg[:, 16 * 65 + h:16 * 65 + h + 1, b_ * SL:(b_ + 1) * SL]
                       .rearrange("r q c -> q r c"), sem="akn")
            P.wait(P.V, e3)
            e4 = P.V.reduce_max(out=km2[0:1, b_:b_ + 1], in_=knrow[0:1, :],
                                axis=AX, _inc=("dv", 1))
            P.wait(P.Y, e4)
            ku_evs += [e1, e1b, e2, e4]
        if h > 0:
            P.wait(P.V, q_ev)
        qc_ev = P.S.copy(out=qunit[0:64, :],
                         in_=qbuf[64 * (h % 2):64 * (h % 2) + 64,
                                  (h // 2) * TOK:(h // 2 + 1) * TOK],
                         _inc=("ev", 1))
        if ow_ev is not None:
            P.wait(P.S, ow_ev)
        sq_ev = P.S.activation(out=sqq[0:64, :],
                               in_=qbuf[64 * (h % 2):64 * (h % 2) + 64,
                                        (h // 2) * TOK:(h // 2 + 1) * TOK],
                               func=AF.Square, _inc=("ev", 1))
        P.wait(P.T, sq_ev, ctx.init_ev)
        if ow_ev is not None:
            P.wait(P.T, ow_ev)
        for nb in range(4):
            n0 = nb * 512
            qmm = P.T.matmul(st4ap[0:1, n0:n0 + 512], ctx.ind_a[0:64, :],
                             sqq[0:64, n0:n0 + 512], start=True, stop=True,
                             _inc=("mm", 1))
            P.wait(P.S, qmm)
            qn_ev = P.S.copy(out=qn2h[0:1, n0:n0 + 512],
                             in_=st4ap[0:1, n0:n0 + 512], _inc=("ev", 1))
        for b_ in range(B):
            P.wait(P.V, P.ev("dv"), qn_ev)
            ee = P.V.tensor_scalar(out=mrow[0:1, :],
                                   in0=qn2h[0:1, b_ * 512:b_ * 512 + 512],
                                   scalar1=km2[0:1, b_:b_ + 1], scalar2=None,
                                   op0=ALU.mult, _inc=("dv", 1))
            P.wait(P.S, ee)
            ee2 = P.S.activation(out=mrow[0:1, :], in_=mrow[0:1, :], func=AF.Sqrt,
                                 _inc=("ev", 1))
            P.wait(P.V, ee2)
            P.V.tensor_scalar(out=mstageb[0:1, b_ * 512:(b_ + 1) * 512],
                              in0=mrow[0:1, :], scalar1=-1.0, scalar2=None,
                              op0=ALU.mult)
            ee3 = P.V.tensor_scalar(out=mstageb[0:1, TOK + b_ * 512:TOK + (b_ + 1) * 512],
                                    in0=mrow[0:1, :], scalar1=-1.0, scalar2=-BIGNEG,
                                    op0=ALU.mult, op1=ALU.add, _inc=("dv", 1))
            P.wait(P.S, ee3)
        mc = ee3
        P.wait(P.G, mc)
        q_ev = P.dma("gpsimd", qunit[64:66, :],
                     mstageb[0:1, :].rearrange("q (a c) -> q a c", a=2),
                     sem="aqr")
        for jj in range(32):
            if dveD is not None and jj < nD:
                if len(dv_hist) >= 2:
                    P.wait(P.Y, dv_hist[-2])
                dt_ev = P.dma("sync", dvet[jj % 2][:, :], dveD[jj, :, :],
                              sem=f"advt{jj % 2}")
            for b_ in range(B):
                P.wait(P.T, *ku_evs)
                P.wait(P.T, q_ev, qu_ms)
                if ow_ev is not None:
                    P.wait(P.T, ow_ev)
                mm_e = P.T.matmul(st4ap[:, b_ * 512:(b_ + 1) * 512],
                                  kunits[b_][:, 128 * jj:128 * jj + 128],
                                  qunit[:, b_ * 512:(b_ + 1) * 512],
                                  start=True, stop=True,
                                  _inc=("mm", 1))
            if dveD is not None and jj < nD:
                P.wait(P.V, mm_e, dt_ev)
                dv_e = P.V.scalar_tensor_tensor(
                    out=st4ap[:, :].rearrange("p (b c) -> p b c", b=4),
                    in0=dvet[jj % 2][:, :].unsqueeze(1)
                    .broadcast_to([128, 4, 512]),
                    scalar=float(cfg.get("dve_scale", [1.0] * NH)[h]),
                    in1=st4ap[:, :].rearrange("p (b c) -> p b c", b=4),
                    op0=ALU.mult, op1=ALU.add, _inc=("dv", 1))
                dv_hist.append(dv_e)
                exp_dep = dv_e
            else:
                exp_dep = mm_e
            P.wait(P.S, exp_dep)
            if len(pv_hist) >= 2:
                P.wait(P.S, pv_hist[-2])
            kw = {"bias": ebt[:, h * 32 + jj:h * 32 + jj + 1]} if ebt is not None else {}
            ex_ev = P.S.activation(out=pt[jj % 2][:, :], in_=st4ap[:, :],
                                   func=AF.Exp, **kw, _inc=("ev", 1))
            P.wait(P.T, ex_ev)
            for b_ in range(B):
                pv_e = P.T.matmul(obs[b_][1][0:65, 0:512],
                                  vunits[b_][:, 65 * jj:65 * jj + 65],
                                  pt[jj % 2][:, b_ * 512:(b_ + 1) * 512],
                                  start=(jj == 0), stop=(jj == 31),
                                  _inc=("mm", 1))
            pv_hist.append(pv_e)
        for b_ in range(B):
            ob_ap = obs[b_][1]
            P.wait(P.V, P.ev("mm"))
            rv = P.V.reciprocal(out=rrow[0:1, :], in_=ob_ap[64:65, 0:512],
                                _inc=("dv", 1))
            P.wait(P.S, rv)
            oc = P.S.copy(out=ostage[:, :], in_=ob_ap[0:64, 0:512], _inc=("ev", 1))
            P.wait(P.T, rv)
            bc = P.T.matmul(bcap[0:64, 0:512], ctx.onesr64[:, :], rrow[0:1, :],
                            start=True, stop=True, _inc=("mm", 1))
            P.wait(P.V, bc, oc)
            ow_ev = P.V.tensor_tensor(
                out=qbuf[64 * (h % 2):64 * (h % 2) + 64,
                         (h // 2) * TOK + b_ * 512:(h // 2) * TOK + b_ * 512 + 512],
                in0=ostage[:, :], in1=bcap[0:64, 0:512], op=ALU.mult,
                _inc=("dv", 1))
            P.wait(P.S, ow_ev)
    psum.release(stb4, P.ev("dv"))
    for b_ in range(B):
        psum.release(obs[b_][0], P.ev("dv"))

    rmw = DramRMW(P, nm + "_res", xr)

    def o_cb(m0, mw, n0, nw, pstile, ev_mm):
        mb = m0 // 128
        if ob_col is not None:
            P.wait(P.V, ev_mm)
            e = P.V.tensor_scalar(out=fst[0:mw, 0:nw], in0=pstile[0:mw, 0:nw],
                                  scalar1=ob_col(mb), scalar2=None,
                                  op0=ALU.add, _inc=("dv", 1))
        else:
            P.wait(P.S, ev_mm)
            e = P.S.activation(out=fst[0:mw, 0:nw], in_=pstile[0:mw, 0:nw],
                               func=AF.Copy, _inc=("ev", 1))
        st = rmw.add_psum(mb, n0, nw, fst[0:mw, 0:nw], e)
        P.wait(P.S, P.ev("dv"))
        return P.ev("dv")

    e_proj_fm(P, cfg["wo"], qbuf, D, D, TOK, o_cb, psum, name=nm + "o",
              x_ev=ow_ev)
    return P.ev("wy")



def e_ffn_local(P, cfg, xint, psum, ctx, lnsp, phase="full"):
    nm = cfg["name"]
    NHL = 8
    sh = P.sb(nm + "_sh", [128, NKC * TOK], BF16)
    qloc = P.sb(nm + "_q", [128, 4 * TOK], BF16)
    kloc = P.sb(nm + "_k", [128, 4 * TOK], BF16)

    def shn_ap(k, n0, nw):
        t = qloc if k < 4 else kloc
        kk = k % 4
        return t[:, kk * TOK + n0:kk * TOK + n0 + nw]
    vcon_s = P.sb(nm + "_vcs", [128, B * NHL * 4 * 65], BF16)
    scr1 = P.sb(nm + "_scr1", [128, TOK], BF16)
    scr2 = P.sb(nm + "_scr2", [128, TOK], BF16)
    nrow = P.sb(nm + "_nrow", [1, TOK], F32)
    mcolf = P.sb(nm + "_mcolf", [128, 1], F32)
    mcolb = P.sb(nm + "_mcolb", [128, 4], BF16)
    qext = P.sb(nm + "_qext", [65, TOK], BF16)
    kunits = [P.sb(f"{nm}_ku{b}", [65, 1024], BF16) for b in range(B)]
    vunits = [P.sb(f"{nm}_vu{b}", [128, 8 * 65], BF16) for b in range(B)]
    pt = [P.sb(f"{nm}_pt{i}", [128, TOK], BF16) for i in range(2)]
    ostage = P.sb(nm + "_ost2", [64, 512], BF16)
    rrow = P.sb(nm + "_rrow2", [1, 512], F32)
    fst = P.sb(nm + "_fst2", [128, 512], F32)
    hstream = [P.sb(f"{nm}_hs{i}", [128, 512], BF16) for i in range(23)]
    hstage = P.sb(nm + "_hstage", [128, TOK], BF16)
    junk = P.sb(nm + "_junk", [1, 16], F32)
    mS_t = [P.sb(f"{nm}_mSt{i}", [128, 1024], BF16) for i in range(2)]
    mT_t = [P.sb(f"{nm}_mTt{i}", [128, 512], BF16) for i in range(2)]
    rott = P.sb(nm + "_rot", [128, 128], BF16)
    tabs = P.sb(nm + "_tabs", [128, 4 * 512], BF16)
    qs8 = P.sb(nm + "_qs8", [128, 2], F32)

    rot_ev = P.dma("gpsimd", rott[:, :], cfg["rotmat"][:, :], sem="cstg")
    tb_ev = P.dma("gpsimd", tabs[:, :], cfg["tabs"][:, :], sem="cstg")
    maskS_d = P.inp(nm + "_maskS", cfg["maskS"], BF16)
    maskT_d = P.inp(nm + "_maskT", cfg["maskT"], BF16)

    hbuf = P.dram(nm + "_hbuf", [2816, TOK], BF16)
    halo_c = P.dram(nm + "_haloc", [512, B], F32)
    halo_g = P.dram(nm + "_halog", [NCORE, 512, B], F32, shared=True)
    kcon_l = P.dram(nm + "_kconl", [NHL * 65, TOK], BF16)
    vcon_l = P.dram(nm + "_vconl", [128, B * NHL * 4 * 65], BF16)
    kg_l = P.dram(nm + "_kgl", [NCORE, NHL * 65, TOK], BF16, shared=True)
    vg_l = P.dram(nm + "_vgl", [NCORE, 128, B * NHL * 4 * 65], BF16, shared=True)
    kgr_l = P.dram(nm + "_kgrl", [NHL * 65, B * 1024], BF16)
    vgr_l = P.dram(nm + "_vgrl", [128, B * NHL * 8 * 65], BF16)

    def build_shift(dst, src_dram, tag2):
        hc_ev = P.dma("sync", halo_c[:, :],
                      src_dram[512:1024, :]
                      .rearrange("p (b c) -> p b c", b=B)[:, :, 511:512]
                      .rearrange("p b one -> p (b one)"),
                      allow_slow_non_contiguous=True, sem="hlc")
        cch = e_gather(P, halo_c, halo_g, [hc_ev], tag2 + "gh")
        evs = []
        for kc in range(4):
            evs.append(P.dma("gpsimd", dst[:, kc * TOK:(kc + 1) * TOK],
                             src_dram[128 * kc:128 * (kc + 1), :], sem="shf"))
        for kc in range(4, 8):
            evs.append(P.dma(
                "gpsimd",
                dst[:, kc * TOK:(kc + 1) * TOK]
                .rearrange("p (b c) -> p b c", b=B)[:, :, 1:512],
                src_dram[128 * kc:128 * (kc + 1), :]
                .rearrange("p (b c) -> p b c", b=B)[:, :, 0:511], sem="shf"))

        def arms(lid):
            descs = []
            if lid == 0:
                for kc in range(4, 8):
                    descs.append(("memset",
                                  dst[:, kc * TOK:(kc + 1) * TOK]
                                  .rearrange("p (b c) -> p b c", b=B)[:, :, 0:1],
                                  0.0))
                    descs.append((junk[0:1, 0:4], halo_g[0, 0:1, :], {}))
            else:
                for kc in range(4, 8):
                    descs.append((
                        dst[:, kc * TOK:(kc + 1) * TOK]
                        .rearrange("p (b c) -> p b c", b=B)[:, :, 0:1],
                        halo_g[lid - 1, 128 * (kc - 4):128 * (kc - 3), :]
                        .unsqueeze(2), {"allow_slow_non_contiguous": True}))
            return descs
        P.wait(P.G, cch)
        sw = P.switch_dma(arms, n_dma=4, arm_engine="gpsimd", sem="shf")
        return evs + [sw]

    rmw = DramRMW(P, nm + "_res", xint)
    last_ev = None
    for li in range(2):
        cl = cfg["subs"][li]
        tag = f"{nm}{li}"
        if li > 0:
            P.barrier_all([P.ev("mm"), P.ev("dv"), P.ev("ev"), P.ev("wy"), P.ev("wg")])
        sh_evs = build_shift(sh, xint, tag + "a")

        def q_cb(m0, mw, n0, nw, pstile, ev_mm):
            mb = m0 // 128
            P.wait(P.S, ev_mm)
            return P.S.activation(out=qloc[0:mw, mb * TOK + n0:mb * TOK + n0 + nw],
                                  in_=pstile[0:mw, 0:nw], func=AF.Copy,
                                  _inc=("ev", 1))

        def k_cb(m0, mw, n0, nw, pstile, ev_mm):
            mb = m0 // 128
            P.wait(P.S, ev_mm)
            return P.S.activation(out=kloc[0:mw, mb * TOK + n0:mb * TOK + n0 + nw],
                                  in_=pstile[0:mw, 0:nw], func=AF.Copy,
                                  _inc=("ev", 1))

        for e in sh_evs:
            P.wait(P.T, e)
        if phase == "sh":
            P.wait(P.G, *sh_evs)
            for k in range(NKC):
                P.dma("gpsimd", xint[128 * k:128 * (k + 1), :], sh[:, k * TOK:(k + 1) * TOK])
            return P.ev("wg")
        e_proj_fm(P, cl["wq"], sh, D, 512, TOK, q_cb, psum, name=tag + "q")
        e_proj_fm(P, cl["wk"], sh, D, 512, TOK, k_cb, psum, name=tag + "k")
        ev0 = P.G.memset(vcon_s[:, :], 0.0)
        ev_ones = P.G.memset(
            vcon_s[:, :].rearrange("p (m s) -> p m s", s=65)[:, :, 64:65], 1.0,
            _inc=("gp", 1))

        def v_cb(ti, m0, mw, pstile, ev_mm):
            b_, jl = ti // 4, ti % 4
            g0 = m0 // 64
            nheads = mw // 64
            P.wait(P.S, ev_mm, ev0, ev_ones)
            mstart = (b_ * NHL + g0) * 4 + jl
            sel = vcon_s[:, :].rearrange("p (m s) -> p m s", s=65)[
                :, mstart:mstart + 4 * (nheads - 1) + 1:4, 0:64]
            return P.S.activation(out=sel,
                                  in_=pstile[0:128, 0:mw].rearrange(
                                      "p (h s) -> p h s", s=64),
                                  func=AF.Copy, _inc=("ev", 1))

        e_proj_tok(P, cl["wv"], sh, D, 512, TOK, v_cb, psum, name=tag + "v")

        if phase == "qkv":
            P.wait(P.G, P.ev("ev"), P.ev("dv"))
            for k in range(4):
                P.dma("gpsimd", xint[128 * k:128 * (k + 1), :], qloc[:, k * TOK:(k + 1) * TOK])
                P.dma("gpsimd", xint[128 * (k + 4):128 * (k + 5), :], kloc[:, k * TOK:(k + 1) * TOK])
            return P.ev("wg")
        qsc_ev = P.dma("sync", qs8[:, :], cl["qsks"][:, :], sem="cst")
        stb, stap, _ = psum.alloc(4, P.T)
        rpb, rpap, _ = psum.alloc(4, P.T)
        for (buf, col, tab0) in ((qloc, 0, 0), (kloc, 1, 2)):
            for kc in range(4):
                P.wait(P.S, P.ev("dv"))
                sq_ev = P.S.activation(out=scr1[:, :], in_=buf[:, kc * TOK:(kc + 1) * TOK],
                                       func=AF.Square, _inc=("ev", 1))
                P.wait(P.T, sq_ev, ctx.init_ev)
                P.wait(P.T, P.ev("ev"))
                for nb in range(4):
                    n0 = nb * 512
                    P.T.matmul(stap[0:1, n0:n0 + 512], ctx.ind_a[:, :],
                               scr1[:, n0:n0 + 512], start=True, stop=True)
                    m2 = P.T.matmul(stap[32:33, n0:n0 + 512], ctx.ind_b[:, :],
                                    scr1[:, n0:n0 + 512], start=True, stop=True,
                                    _inc=("mm", 1))
                P.wait(P.S, m2)
                for half in range(2):
                    srow = stap[0:1, 0:TOK] if half == 0 else stap[32:33, 0:TOK]
                    e1 = P.S.activation(out=nrow[0:1, 0:TOK], in_=srow,
                                        func=AF.Sqrt, _inc=("ev", 1))
                    P.wait(P.V, e1)
                    P.V.tensor_scalar(out=nrow[0:1, 0:TOK], in0=nrow[0:1, 0:TOK],
                                      scalar1=1e-12, scalar2=None, op0=ALU.max)
                    e2 = P.V.reciprocal(out=nrow[0:1, 0:TOK], in_=nrow[0:1, 0:TOK],
                                        _inc=("dv", 1))
                    P.wait(P.T, e2)
                    for nb in range(4):
                        n0 = nb * 512
                        bce = P.T.matmul(rpap[64 * half:64 * half + 64, n0:n0 + 512],
                                         lnsp.ones_row[0:1, 0:64],
                                         nrow[0:1, n0:n0 + 512],
                                         start=True, stop=True, _inc=("mm", 1))
                    P.wait(P.S, bce)
                P.wait(P.V, bce)
                e3 = P.V.tensor_tensor(out=scr1[:, :], in0=buf[:, kc * TOK:(kc + 1) * TOK],
                                       in1=rpap[:, 0:TOK], op=ALU.mult)
                P.wait(P.V, qsc_ev)
                e4 = P.V.tensor_scalar(out=buf[:, kc * TOK:(kc + 1) * TOK],
                                       in0=scr1[:, :], scalar1=qs8[:, col:col + 1],
                                       scalar2=None, op0=ALU.mult, _inc=("dv", 1))
                P.wait(P.T, e4, rot_ev)
                for nb in range(4):
                    n0 = nb * 512
                    rote = P.T.matmul(rpap[:, n0:n0 + 512], rott[:, :],
                                      buf[:, kc * TOK + n0:kc * TOK + n0 + 512],
                                      start=True, stop=True, _inc=("mm", 1))
                P.wait(P.V, rote, tb_ev)
                P.V.tensor_tensor(
                    out=scr1[:, :].rearrange("p (b c) -> p b c", b=B),
                    in0=buf[:, kc * TOK:(kc + 1) * TOK].rearrange("p (b c) -> p b c", b=B),
                    in1=tabs[:, tab0 * 512:tab0 * 512 + 512].unsqueeze(1)
                    .broadcast_to([128, B, 512]), op=ALU.mult)
                P.V.tensor_tensor(
                    out=scr2[:, :].rearrange("p (b c) -> p b c", b=B),
                    in0=rpap[:, 0:TOK].rearrange("p (b c) -> p b c", b=B),
                    in1=tabs[:, (tab0 + 1) * 512:(tab0 + 1) * 512 + 512].unsqueeze(1)
                    .broadcast_to([128, B, 512]), op=ALU.mult)
                e5 = P.V.tensor_tensor(out=buf[:, kc * TOK:(kc + 1) * TOK],
                                       in0=scr1[:, :], in1=scr2[:, :], op=ALU.add,
                                       _inc=("dv", 1))
                P.wait(P.T, e5)
        psum.release(stb, P.ev("dv"))
        psum.release(rpb, P.ev("dv"))

        if phase == "rot":
            P.wait(P.G, P.ev("ev"), P.ev("dv"))
            for k in range(4):
                P.dma("gpsimd", xint[128 * k:128 * (k + 1), :], qloc[:, k * TOK:(k + 1) * TOK])
                P.dma("gpsimd", xint[128 * (k + 4):128 * (k + 5), :], kloc[:, k * TOK:(k + 1) * TOK])
            return P.ev("wg")
        kdma = []
        P.wait(P.Y, P.ev("dv"), ctx.init_ev)
        for h in range(NHL):
            kc_, half = h // 2, h % 2
            kdma.append(P.dma("sync", kcon_l[65 * h:65 * h + 64, :],
                              kloc[64 * half:64 * half + 64, kc_ * TOK:(kc_ + 1) * TOK],
                              sem="fkc"))
            kdma.append(P.dma("sync", kcon_l[65 * h + 64:65 * h + 65, :],
                              ctx.ones2k[:, :], sem="fkc"))
        P.wait(P.Y, P.ev("ev"))
        vdma = [P.dma("sync", vcon_l[:, :], vcon_s[:, :], sem="fvc")]
        cck = e_gather(P, kcon_l, kg_l, kdma, tag + "gk")
        ccv = e_gather(P, vcon_l, vg_l, vdma, tag + "gv")

        def arms2(lid):
            descs = []
            rlo = lid - 1 if lid > 0 else 0
            for b_ in range(B):
                for (dsti, r_) in ((0, rlo), (1, lid)):
                    descs.append((
                        kgr_l[:, b_ * 1024 + dsti * 512:b_ * 1024 + dsti * 512 + 512],
                        kg_l[r_, :, b_ * SL:(b_ + 1) * SL], {}))
                    descs.append((
                        vgr_l[:, :].rearrange("p (bb h j s) -> p bb h (j s)",
                                              bb=B, h=NHL, j=8)[
                            :, b_, :, dsti * 4 * 65:(dsti + 1) * 4 * 65],
                        vg_l[r_, :, :].rearrange("p (bb h c) -> p bb h c",
                                                 bb=B, h=NHL, c=4 * 65)[:, b_, :, :],
                        {}))
            return descs

        P.wait(P.G, cck, ccv)
        rear2 = P.switch_dma(arms2, n_dma=B * 4, arm_engine="gpsimd", sem="rear")

        stb4, st4ap, _ = psum.alloc(4, P.T)
        obs = [psum.alloc(1, P.T) for _ in range(B)]
        mtap = st4ap[:, 1536:2048]
        mrow4 = P.sb(tag + "_mrow4", [4, 128], BF16)
        pv_hist = []
        ow_ev = None
        for h in range(NHL):
            kc_, half = h // 2, h % 2
            P.wait(P.Y, rear2)
            P.wait(P.Y, P.ev("mm"))
            P.wait(P.G, P.ev("mm"))
            P.wait(P.S, P.ev("mm"))
            P.wait(P.T, P.ev("dv"))
            ku_evs = []
            for b_ in range(B):
                ku_evs.append(P.dma("sync", kunits[b_][:, :],
                                    kgr_l[65 * h:65 * h + 65,
                                          b_ * 1024:(b_ + 1) * 1024], sem="fku"))
                ku_evs.append(P.dma("sync", vunits[b_][:, :],
                                    vgr_l[:, :].rearrange("p (bb h c) -> p bb h c",
                                                          bb=B, h=NHL, c=8 * 65)[
                                        :, b_, h, :], sem="fku"))
            qc_ev = P.S.copy(out=qext[0:64, :],
                             in_=qloc[64 * half:64 * half + 64,
                                      kc_ * TOK:(kc_ + 1) * TOK], _inc=("ev", 1))
            for b_ in range(B):
                P.wait(P.T, *ku_evs)
                P.wait(P.T, qc_ev, ctx.init_ev)
                for qc in range(4):
                    sp_ps = st4ap[:, 0:1024]
                    lhs = qext[0:64, b_ * 512 + qc * 128:b_ * 512 + qc * 128 + 128]
                    P.wait(P.Y, P.ev("dv"))
                    mse = P.dma("sync", mS_t[qc % 2][:, :], maskS_d[qc, :, :],
                                sem="fmsk")
                    P.T.matmul(sp_ps[:, 0:512], lhs, kunits[b_][0:64, 0:512],
                               start=True, stop=True)
                    sme = P.T.matmul(sp_ps[:, 512:1024], lhs,
                                     kunits[b_][0:64, 512:1024],
                                     start=True, stop=True, _inc=("mm", 1))
                    P.wait(P.V, sme, mse)
                    P.V.tensor_tensor(out=sp_ps, in0=mS_t[qc % 2][:, :],
                                      in1=sp_ps, op=ALU.add)
                    P.V.reduce_max(out=mcolf[:, 0:1], in_=sp_ps, axis=AX)
                    ce = P.V.tensor_scalar(out=mcolb[:, qc:qc + 1], in0=mcolf[:, 0:1],
                                           scalar1=-1.0, scalar2=None, op0=ALU.mult,
                                           _inc=("dv", 1))
                    P.wait(P.T, ce)
                tp = P.T.transpose(mtap[:, 0:256].bitcast(BF16)[0:4, 0:128],
                                   mcolb[:, 0:4], ctx.eye_b[:, :], _inc=("mm", 1))
                P.wait(P.S, tp)
                me = P.S.copy(out=mrow4[:, :],
                              in_=mtap[:, 0:256].bitcast(BF16)[0:4, 0:128],
                              _inc=("ev", 1))
                P.wait(P.G, me)
                mde = P.dma("gpsimd",
                            qext[64:65, b_ * 512:(b_ + 1) * 512]
                            .rearrange("q (a c) -> q a c", a=4),
                            mrow4[:, :], sem="fqr")
                P.wait(P.T, mde)
                P.wait(P.V, mde)
                P.wait(P.S, mde)
            q_ev = P.ev("wg")
            for jj in range(8):
                P.wait(P.Y, P.ev("dv"))
                mte = P.dma("sync", mT_t[jj % 2][:, :], maskT_d[jj, :, :],
                            sem="fmt")
                for b_ in range(B):
                    P.wait(P.T, q_ev)
                    mm_e = P.T.matmul(st4ap[:, b_ * 512:(b_ + 1) * 512],
                                      kunits[b_][:, 128 * jj:128 * jj + 128],
                                      qext[:, b_ * 512:(b_ + 1) * 512],
                                      start=True, stop=True, _inc=("mm", 1))
                P.wait(P.V, mm_e, mte)
                dv_e = P.V.tensor_tensor(
                    out=st4ap[:, :].rearrange("p (b c) -> p b c", b=B),
                    in0=mT_t[jj % 2][:, :].unsqueeze(1)
                    .broadcast_to([128, B, 512]),
                    in1=st4ap[:, :].rearrange("p (b c) -> p b c", b=B),
                    op=ALU.add, _inc=("dv", 1))
                P.wait(P.S, dv_e)
                if len(pv_hist) >= 2:
                    P.wait(P.S, pv_hist[-2])
                ex_ev = P.S.activation(out=pt[jj % 2][:, :], in_=st4ap[:, :],
                                       func=AF.Exp, _inc=("ev", 1))
                P.wait(P.T, ex_ev)
                for b_ in range(B):
                    pv_e = P.T.matmul(obs[b_][1][0:65, 0:512],
                                      vunits[b_][:, 65 * jj:65 * jj + 65],
                                      pt[jj % 2][:, b_ * 512:(b_ + 1) * 512],
                                      start=(jj == 0), stop=(jj == 7),
                                      _inc=("mm", 1))
                pv_hist.append(pv_e)
            for b_ in range(B):
                ob_ap = obs[b_][1]
                P.wait(P.V, P.ev("mm"))
                rv = P.V.reciprocal(out=rrow[0:1, :], in_=ob_ap[64:65, 0:512],
                                    _inc=("dv", 1))
                P.wait(P.S, rv)
                oc = P.S.copy(out=ostage[:, :], in_=ob_ap[0:64, 0:512], _inc=("ev", 1))
                P.wait(P.T, rv)
                bc = P.T.matmul(mtap[0:64, 0:512], ctx.onesr64[:, :], rrow[0:1, :],
                                start=True, stop=True, _inc=("mm", 1))
                P.wait(P.V, bc, oc)
                ow_ev = P.V.tensor_tensor(
                    out=qloc[64 * half:64 * half + 64,
                             kc_ * TOK + b_ * 512:kc_ * TOK + b_ * 512 + 512],
                    in0=ostage[:, :], in1=mtap[0:64, 0:512], op=ALU.mult,
                    _inc=("dv", 1))
                P.wait(P.S, ow_ev)
        psum.release(stb4, P.ev("dv"))
        for b_ in range(B):
            psum.release(obs[b_][0], P.ev("dv"))

        def o_cb(m0, mw, n0, nw, pstile, ev_mm):
            mb = m0 // 128
            P.wait(P.S, ev_mm)
            e = P.S.activation(out=fst[0:mw, 0:nw], in_=pstile[0:mw, 0:nw],
                               func=AF.Copy, _inc=("ev", 1))
            rmw.add_psum(mb, n0, nw, fst[0:mw, 0:nw], e)
            P.wait(P.S, P.ev("dv"))
            return P.ev("dv")

        e_proj_fm(P, cl["wo"], qloc, 512, D, TOK, o_cb, psum, name=tag + "o",
                  x_ev=ow_ev)
        P.barrier_all([P.ev("wy"), P.ev("dv"), P.ev("mm"), P.ev("ev")])
        if phase == "mid":
            return P.ev("wy")
        if phase == "sub0dump" and li == 0:
            dmid = P.dout(nm + "_dmid", [D, TOK], F32)
            dm_ev = P.dma("sync", dmid[:, :], xint[:, :])
            P.wait(P.Y, dm_ev)
            P.barrier_all([dm_ev])

        sh2_evs = build_shift(sh, xint, tag + "f")
        lng_t = P.sb(tag + "_lng", [128, 8], F32)
        lnb_t = P.sb(tag + "_lnb", [128, 8], F32)
        b1_t = P.sb(tag + "_b1", [128, 44], F32)
        b2_t = P.sb(tag + "_b2", [128, 8], F32)
        e_g = P.dma("sync", lng_t[:, :], cl["lng"][:, :], sem="cst")
        e_g = P.dma("sync", lnb_t[:, :], cl["lnb"][:, :], sem="cst")
        e_g = P.dma("sync", b1_t[:, :], cl["b1i"][:, :], sem="cst")
        e_g = P.dma("sync", b2_t[:, :], cl["b2"][:, :], sem="cst")
        P.wait(P.V, e_g)
        for e in sh2_evs:
            P.wait(P.S, e)
            P.wait(P.T, e)
            P.wait(P.V, e)

        def wout(k, n0, nw, tmp):
            return P.V.tensor_scalar(out=shn_ap(k, n0, nw),
                                     in0=tmp, scalar1=lng_t[:, k:k + 1],
                                     scalar2=lnb_t[:, k:k + 1],
                                     op0=ALU.mult, op1=ALU.add, _inc=("dv", 1))

        def xsrc2(k, n0, nw):
            return sh[:, k * TOK + n0:k * TOK + n0 + nw], None

        ln_ev = e_layernorm(P, lnsp, xsrc2, TOK, wout, bf_in=True)
        P.wait(P.T, ln_ev)
        P.wait(P.S, ln_ev)
        if phase == "ln":
            P.wait(P.G, ln_ev)
            for k in range(NKC):
                P.dma("gpsimd", xint[128 * k:128 * (k + 1), :], shn_ap(k, 0, TOK))
            return P.ev("wg")

        wst = {"a_ev": None, "m_ev": None}

        def h_cb(m0, mw, n0, nw, pstile, ev_mm):
            mb = m0 // 128
            hc = mb // 2
            if mb % 2 == 0:
                P.wait(P.V, ev_mm)
                e = P.V.tensor_scalar(out=scr1[:, n0:n0 + nw], in0=pstile[0:mw, 0:nw],
                                      scalar1=b1_t[:, mb:mb + 1], scalar2=None,
                                      op0=ALU.add, _inc=("dv", 1))
                wst["a_ev"] = e
                return e
            P.wait(P.S, ev_mm)
            if wst["m_ev"] is not None:
                P.wait(P.S, wst["m_ev"])
            e1 = P.S.activation(out=scr2[:, n0:n0 + nw], in_=pstile[0:mw, 0:nw],
                                func=AF.Gelu, bias=b1_t[:, mb:mb + 1], _inc=("ev", 1))
            P.wait(P.V, e1, wst["a_ev"])
            e2 = P.V.tensor_tensor(out=hstage[:, n0:n0 + nw], in0=scr1[:, n0:n0 + nw],
                                   in1=scr2[:, n0:n0 + nw], op=ALU.mult, _inc=("dv", 1))
            wst["m_ev"] = e2
            P.wait(P.Y, e2)
            st = P.dma("sync", hbuf[128 * hc:128 * (hc + 1), n0:n0 + nw],
                       hstage[:, n0:n0 + nw], sem="fhb")
            P.wait(P.V, st)
            P.wait(P.S, st)
            return e2

        e_proj_fm(P, cl["w1i"], shn_ap, D, 5632, TOK, h_cb, psum, name=tag + "w1")
        h_done = P.ev("wy")
        if phase == "hall":
            hd = P.dout(nm + "_hall", [2816, TOK], BF16)
            P.wait(P.G, h_done)
            for k in range(22):
                P.dma("gpsimd", hd[128 * k:128 * (k + 1), :],
                      hbuf[128 * k:128 * (k + 1), :])
            return P.ev("wg")
        if phase == "sub0dump" and li == 0:
            hd = P.dout(nm + "_hall", [2816, TOK], BF16)
            hd_ev = None
            P.wait(P.Y, h_done)
            for k in range(22):
                hd_ev = P.dma("sync", hd[128 * k:128 * (k + 1), :],
                              hbuf[128 * k:128 * (k + 1), :])
            P.barrier_all([hd_ev])
        if phase in ("h1", "h2", "h3"):
            off = {"h1": 0, "h2": 8, "h3": 14}[phase]
            P.wait(P.G, h_done)
            for k in range(NKC):
                P.dma("gpsimd", xint[128 * k:128 * (k + 1), :],
                      hbuf[128 * (k + off):128 * (k + off + 1), :])
            return P.ev("wg")

        hs_state = {"i": 0, "guard": [None, None, None]}

        def xsrc3(k, n0, nw):
            i = hs_state["i"] % 23
            hs_state["i"] += 1
            if hs_state["i"] > 23:
                P.wait(P.G, P.ev("mm"))
            P.wait(P.G, h_done)
            ld = P.dma("gpsimd", hstream[i][:, 0:nw],
                       hbuf[128 * k:128 * k + 128, n0:n0 + nw],
                       sem=f"fhs{i}")
            return hstream[i][:, 0:nw], ld

        def w2_cb(m0, mw, n0, nw, pstile, ev_mm):
            mb = m0 // 128
            P.wait(P.V, ev_mm)
            e = P.V.tensor_scalar(out=fst[0:mw, 0:nw], in0=pstile[0:mw, 0:nw],
                                  scalar1=b2_t[:, mb:mb + 1], scalar2=None,
                                  op0=ALU.add, _inc=("dv", 1))
            rmw.add_psum(mb, n0, nw, fst[0:mw, 0:nw], e)
            P.wait(P.V, P.ev("wy"))
            return P.ev("dv")

        ws2 = WeightStream(P, cl["w2p"], 2816, D, mgw=512, name=tag + "w2")
        pool2 = PsumPool(P, psum, 3)
        e_matmul_stack(P, ws2, xsrc3, TOK, w2_cb, pool2)
        pool2.free_all()
        last_ev = P.ev("wy")
        if phase in ("sub0", "sub0dump"):
            return last_ev
    return last_ev



def col_form(b, pad_cols=None):
    b = np.asarray(b, np.float32).ravel()
    n = cdiv(b.size, 128) * 128
    if b.size < n:
        b = np.concatenate([b, np.zeros(n - b.size, np.float32)])
    a = np.ascontiguousarray(b.reshape(-1, 128).T)
    if pad_cols is not None and a.shape[1] < pad_cols:
        a = np.concatenate([a, np.zeros((128, pad_cols - a.shape[1]), np.float32)], 1)
    return a


def attn_host_consts(l):
    slopes = alibi_slopes(16)
    tks = np.arange(128)
    flags, dves, ebs = [], [], []
    for c in range(NCORE):
        t0 = SL * c
        tqs = np.arange(t0, t0 + SL)
        flag = np.zeros(T, np.float32)
        if l in (1, 3):
            for jj in range(32):
                g = (4 * c + jj) % 32
                tkg = 128 * g + tks
                flag[128 * jj:128 * (jj + 1)] = (tkg >= t0 + SL).astype(np.float32)
        flags.append(flag.reshape(1, T).astype(BF))
        if l == 0:
            Ds = []
            for jj in range(32):
                g = (4 * c + jj) % 32
                dd = np.minimum((128 * g + tks)[:, None] - tqs[None, :], 0)
                Ds.append(dd.astype(np.float32))
            dves.append(np.stack(Ds).astype(BF))
        else:
            Ds = []
            for jj in range(4):
                dd = np.where((128 * jj + tks)[:, None] > np.arange(SL)[None, :],
                              -BIGNEG, 0.0)
                Ds.append(dd.astype(np.float32))
            dves.append(np.stack(Ds).astype(BF))
        if l == 3:
            eb = np.zeros((128, 16 * 32), np.float32)
            for h in range(16):
                for jj in range(32):
                    g = (4 * c + jj) % 32
                    eb[:, h * 32 + jj] = -slopes[h] * (128 * g + tks)
            ebs.append(eb)
    out = {"flag_rot": flags, "dve_add": dves,
           "dve_scale": [float(s) for s in slopes] if l == 0 else [1.0] * 16,
           "expbias": ebs if l == 3 else None}
    return out


def build_attn_prog(l, weights):
    P = Prog()
    x = P.din("x", [D, TOK])
    y = P.dout("y", [D, TOK])
    xnout = P.dout("xn", [D, TOK], BF16)
    wq = P.inp("wq", weights["wq"]); wk = P.inp("wk", weights["wk"])
    wv = P.inp("wv", weights["wv"]); wo = P.inp("wo", weights["wo"])
    xn = P.sb("xnt", [128, NKC * TOK], BF16)
    ones128 = P.sb("ones128", [1, 128], BF16)
    P.G.memset(ones128[:, :], 1.0)
    cols = {}
    for nm_ in ("qb", "kb", "ob"):
        if weights.get(nm_) is not None:
            dap = P.inp(nm_, weights[nm_])
            tl = P.sb(nm_ + "_t", [128, 8], F32)
            ev = P.dma("sync", tl[:, :], dap[:, :], sem="cst")
            cols[nm_] = tl
    vbt = None
    if weights.get("vb") is not None:
        vbd = P.inp("vb", weights["vb"])
        vbt = P.sb("vb_t", [1, D], BF16)
        ev = P.dma("gpsimd", vbt[:, :], vbd[:, :], sem="cstg")
    x_ev = P.dma("sync", y[:, :], x[:, :], sem="cst")
    P.barrier_all([P.ev("cst")])
    if "cstg" in P.cnt:
        P.barrier_all([P.ev("cstg")])
    qsc = None
    if "qb" in cols:
        qsc = P.V.tensor_scalar(out=cols["qb"][:, :], in0=cols["qb"][:, :],
                                scalar1=0.125, scalar2=None, op0=ALU.mult,
                                _inc=("dv", 1))
    psum = Psum(P)
    lnsp = NormScratch(P, psum)
    ctx = AttnCtx(P, psum, "ax")
    xsrc = DramSrc(P, "xs", y)

    def wout(k, n0, nw, tmp):
        return P.V.tensor_copy(out=xn[:, k * TOK + n0:k * TOK + n0 + nw],
                               in_=tmp, _inc=("dv", 1))

    ln_ev = e_layernorm(P, lnsp, xsrc.get, TOK, wout)
    P.wait(P.T, ln_ev, qsc)
    P.wait(P.S, ln_ev)
    consts = attn_host_consts(l)
    cfg = dict(name="ax", wq=wq, wk=wk, wv=wv, wo=wo,
               qb_col=(lambda mb: cols["qb"][:, mb:mb + 1]) if "qb" in cols else None,
               kb_col=(lambda mb: cols["kb"][:, mb:mb + 1]) if "kb" in cols else None,
               ob_col=(lambda mb: cols["ob"][:, mb:mb + 1]) if "ob" in cols else None,
               v_extra=(ones128[:, :], vbt) if vbt is not None else None,
               **consts)
    aev = e_attn_big(P, cfg, xn, y, psum, ctx)
    P.wait(P.G, P.ev("dv"))
    for k in range(NKC):
        xne = P.dma("gpsimd", xnout[128 * k:128 * (k + 1), :],
                    xn[:, k * TOK:(k + 1) * TOK], sem="xout")
    P.wait(P.G, aev, xne)
    P.wait(P.Y, aev)
    nc = P.finalize()
    return P, nc


def ffn_host_cfg_arrays(wqkv, q_scale, k_scale, wo, ln_g, ln_b, w1, b1, w2, b2):
    HID = 2730
    qc, qs, kc, ks = rotary_tables()
    shared = {}
    for l in range(2):
        shared[f"wq{l}"] = np.ascontiguousarray(wqkv[l][:, 0:512])
        shared[f"wk{l}"] = np.ascontiguousarray(wqkv[l][:, 512:1024])
        shared[f"wv{l}"] = np.ascontiguousarray(wqkv[l][:, 1024:1536])
        qsks = np.zeros((128, 2), np.float32)
        qsks[:, 0] = np.tile(q_scale[l], 2) * 8.0
        qsks[:, 1] = np.tile(k_scale[l], 2)
        shared[f"qsks{l}"] = qsks
        shared[f"wo{l}"] = np.ascontiguousarray(wo[l])
        shared[f"lng{l}"] = col_form(ln_g[l])
        shared[f"lnb{l}"] = col_form(ln_b[l])
        w1a, w1g = w1[l][:, 0:HID], w1[l][:, HID:]
        w1i = np.zeros((D, 5632), np.float32)
        b1i = np.zeros((128, 44), np.float32)
        for hc in range(22):
            aw = w1a[:, 128 * hc:128 * (hc + 1)]
            gw = w1g[:, 128 * hc:128 * (hc + 1)]
            w1i[:, 256 * hc:256 * hc + aw.shape[1]] = aw
            w1i[:, 256 * hc + 128:256 * hc + 128 + gw.shape[1]] = gw
            ab = b1[l][128 * hc:128 * (hc + 1)]
            gb = b1[l][HID + 128 * hc:HID + 128 * (hc + 1)]
            b1i[0:ab.size, 2 * hc] = ab
            b1i[0:gb.size, 2 * hc + 1] = gb
        shared[f"w1i{l}"] = w1i
        shared[f"b1i{l}"] = b1i
        w2p = np.zeros((2816, D), np.float32)
        w2p[0:HID] = w2[l]
        shared[f"w2p{l}"] = w2p
        shared[f"b2{l}"] = col_form(b2[l])
    R = np.zeros((128, 128), np.float32)
    for blk in range(2):
        o = 64 * blk
        for m in range(32):
            R[o + m + 32, o + m] = -1.0
        for m in range(32, 64):
            R[o + m - 32, o + m] = 1.0
    shared["rotmat"] = R
    tabs_l, maskS_l, maskT_l = [], [], []
    for c in range(NCORE):
        t0 = SL * c
        tb = np.zeros((128, 4 * 512), np.float32)
        for i, tab in enumerate((qc, qs, kc, ks)):
            tb[:, 512 * i:512 * (i + 1)] = np.tile(tab[:, t0:t0 + SL], (2, 1))
        tabs_l.append(tb)
        mS = np.zeros((4, 128, 1024), np.float32)
        for qcb in range(4):
            tq = t0 + qcb * 128 + np.arange(128)
            tk = SL * (c - 1) + np.arange(1024)
            ok = (tk[None, :] <= tq[:, None]) & (tk[None, :] >= tq[:, None] - 512)
            if c == 0:
                ok &= (np.arange(1024)[None, :] >= 512)
            mS[qcb] = np.where(ok, 0.0, -BIGNEG)
        maskS_l.append(mS.astype(BF))
        mT = np.zeros((8, 128, 512), np.float32)
        for jj in range(8):
            tk = SL * (c - 1) + jj * 128 + np.arange(128)
            tq = t0 + np.arange(512)
            ok = (tk[:, None] <= tq[None, :]) & (tk[:, None] >= tq[None, :] - 512)
            if c == 0 and jj < 4:
                ok &= False
            mT[jj] = np.where(ok, 0.0, -BIGNEG)
        maskT_l.append(mT.astype(BF))
    percore = {"tabs": tabs_l, "maskS": maskS_l, "maskT": maskT_l}
    return shared, percore


def build_ffn_prog(ffnw):
    shared, percore = ffnw
    P = Prog()
    xin = P.din("x", [D, TOK])
    xnin = P.din("xnb", [D, TOK], BF16)
    y = P.dout("y", [D, TOK])
    xint = P.dram("xint", [D, TOK], F32)
    upev = P.dma("gpsimd", xint[:, :], xnin[:, :], sem="cstg")
    P.barrier_all([upev])
    psum = Psum(P)
    lnsp = NormScratch(P, psum)
    ctx = AttnCtx(P, psum, "fx")
    subs = []
    for l in range(2):
        cl = {}
        for nm_ in ("wq", "wk", "wv", "wo", "qsks", "lng", "lnb", "w1i", "b1i",
                    "w2p", "b2"):
            cl[nm_] = P.inp(f"f_{nm_}{l}", shared[f"{nm_}{l}"])
        subs.append(cl)
    rot_d = P.inp("f_rot", shared["rotmat"])
    tabs_d = P.inp("f_tabs", percore["tabs"])
    cfg = dict(name="fx", rotmat=rot_d, tabs=tabs_d,
               maskS=percore["maskS"], maskT=percore["maskT"], subs=subs)
    fev = e_ffn_local(P, cfg, xint, psum, ctx, lnsp)
    P.barrier_all([fev, P.ev("dv"), P.ev("mm"), P.ev("ev")])
    fadd = [P.sb(f"fa{i}", [128, 512], F32) for i in range(2)]
    fadd2 = [P.sb(f"fb{i}", [128, 512], F32) for i in range(2)]
    lastst = [None, None]
    sev = None
    for k in range(NKC):
        for nb in range(4):
            i = (k * 4 + nb) % 2
            if lastst[i] is not None:
                P.wait(P.Y, lastst[i])
            l1 = P.dma("sync", fadd[i][:, :],
                       xin[128 * k:128 * (k + 1), 512 * nb:512 * (nb + 1)],
                       sem=f"fad{i}")
            l2 = P.dma("sync", fadd2[i][:, :],
                       xint[128 * k:128 * (k + 1), 512 * nb:512 * (nb + 1)],
                       sem=f"fad{i}")
            P.wait(P.V, l2)
            ae = P.V.tensor_tensor(out=fadd[i][:, :], in0=fadd[i][:, :],
                                   in1=fadd2[i][:, :], op=ALU.add, _inc=("dv", 1))
            P.wait(P.Y, ae)
            sev = P.dma("sync", y[128 * k:128 * (k + 1), 512 * nb:512 * (nb + 1)],
                        fadd[i][:, :], sem=f"fad{i}")
            lastst[i] = sev
    P.wait(P.G, sev)
    P.wait(P.Y, sev)
    for i in (0, 1):
        P.wait(P.Y, lastst[i])
        P.wait(P.G, lastst[i])
    nc = P.finalize()
    return P, nc


def prep_attn_weights(l, inp):
    C = np.ascontiguousarray
    if l == 0:
        W = inp["L0_attn_W_qkv"]; b = inp["L0_attn_b_qkv"]
        return dict(wq=C(W[:, 0:D]), wk=C(W[:, D:2 * D]), wv=C(W[:, 2 * D:3 * D]),
                    qb=col_form(b[0:D]), kb=col_form(b[D:2 * D]),
                    vb=C(b[2 * D:3 * D].reshape(1, D)),
                    wo=C(inp["L0_attn_W_proj"]), ob=col_form(inp["L0_attn_b_proj"]))
    if l == 1:
        W = inp["L1_attn_Wqkv"]
        return dict(wq=C(W[0:D].T), wk=C(W[D:2 * D].T), wv=C(W[2 * D:3 * D].T),
                    qb=None, kb=None, vb=None,
                    wo=C(inp["L1_attn_Wout"].T), ob=col_form(inp["L1_attn_bout"]))
    if l == 3:
        return dict(wq=C(inp["L3_attn_Wq"]), wk=C(inp["L3_attn_Wk"]),
                    wv=C(inp["L3_attn_Wv"]), wo=C(inp["L3_attn_Wo"]),
                    qb=col_form(inp["L3_attn_bq"]), kb=col_form(inp["L3_attn_bk"]),
                    vb=C(inp["L3_attn_bv"].reshape(1, D)),
                    ob=col_form(inp["L3_attn_bo"]))
    raise ValueError(l)


_PROGS = {}


def get_prog(key, builder, *args):
    if key not in _PROGS:
        _PROGS[key] = builder(*args)
    return _PROGS[key]


def run_prog(P, nc, per_core_inputs, extra=None):
    in_maps = []
    for c in range(NCORE):
        m = {}
        for nm_, arrs in {**P.input_arrays, **(extra or {}), **per_core_inputs}.items():
            m[nm_] = arrs[c] if isinstance(arrs, list) else arrs
        in_maps.append(m)
    res = run_bass_kernel_spmd(nc, in_maps, core_ids=list(range(NCORE)))
    return res.results


def shard_fm(xfull):
    out = []
    for c in range(NCORE):
        xs = xfull[:, SL * c:SL * (c + 1), :]
        out.append(np.ascontiguousarray(xs.reshape(B * SL, D).T))
    return out


def unshard_fm(shards):
    xs = [np.asarray(s).T.reshape(B, SL, D) for s in shards]
    return np.concatenate(xs, axis=1)


def run_layer_attn(l, weights, x_shards):
    P, nc = get_prog(("attn", l), build_attn_prog, l, weights)
    res = run_prog(P, nc, {"x": x_shards},
                   extra={k: v for k, v in weights.items() if v is not None})
    return ([r["y"] for r in res], [r["xn"] for r in res])


def run_layer_ffn(ffnw, x_shards, xn_shards):
    P, nc = get_prog(("ffn", id(ffnw)), build_ffn_prog, ffnw)
    res = run_prog(P, nc, {"x": x_shards, "xnb": xn_shards})
    return [r["y"] for r in res]



def e_gate_rows(P, psum, ctx, xn_bf, wg_dram, nE, gb_bcast, gmode, grow_tiles,
                name="gate"):
    sc = P.sb(name + "_sc", [128, nE], F32)
    orig = P.sb(name + "_orig", [128, nE], F32)
    t1 = P.sb(name + "_t1", [128, nE], F32)
    oh1f = P.sb(name + "_oh1f", [128, nE], F32)
    oh2f = P.sb(name + "_oh2f", [128, nE], F32)
    col1 = P.sb(name + "_c1", [128, 1], F32)
    col2 = P.sb(name + "_c2", [128, 1], F32)
    col3 = P.sb(name + "_c3", [128, 1], F32)
    gcol = P.sb(name + "_gc", [128, 4], F32)
    gcol2 = P.sb(name + "_gc2", [128, 4], F32)
    comb = P.sb(name + "_comb", [128, nE], BF16)
    zz = P.sb(name + "_zz", [128, nE], F32)
    bcl = P.sb(name + "_bcl", [128, nE], F32)
    mze = P.G.memset(zz[:, :], 0.0, _inc=("gp", 1))
    combc = P.sb(name + "_combc", [128, 1], BF16)
    gstagE = P.sb(name + "_gstE", [nE, 128], BF16)

    def cb(ti, m0, mw, pstile, ev_mm):
        P.wait(P.V, ev_mm)
        P.wait(P.V, mze)
        P.wait(P.V, P.ev("ev"))
        if gmode == "l3" and gb_bcast is not None:
            ae = P.V.tensor_tensor(out=sc[:, :], in0=pstile[0:128, 0:nE],
                                   in1=gb_bcast, op=ALU.add, _inc=("dv", 1))
            src_ap = sc[:, :]
        else:
            ae = P.V.tensor_copy(out=sc[:, :], in_=pstile[0:128, 0:nE],
                                 _inc=("dv", 1))
            src_ap = sc[:, :]
        P.wait(P.V, ae)
        if _os.environ.get("MOE_DBG2") == "logits" and P._gdbg is not None:
            P.wait(P.S, ae)
            P.S.copy(out=P._gdbg[:, ti * nE:(ti + 1) * nE], in_=sc[:, :],
                     _inc=("ev", 1))
            P.wait(P.V, P.ev("ev"))
        def vx(meth, **kw):
            e = getattr(P.V, meth)(**kw, _inc=("dv", 1))
            P.wait(P.V, e)
            return e

        vx("reduce_max", out=col1[:, :], in_=src_ap, axis=AX)
        se = vx("tensor_scalar", out=t1[:, :], in0=src_ap,
                scalar1=col1[:, :], scalar2=None, op0=ALU.subtract)
        P.wait(P.S, se)
        ee = P.S.activation(out=sc[:, :], in_=t1[:, :], func=AF.Exp,
                            _inc=("ev", 1))
        P.wait(P.V, ee)
        vx("reduce_sum", out=col3[:, :], in_=sc[:, :], axis=AX)
        vx("reciprocal", out=col2[:, :], in_=col3[:, :])
        vx("tensor_scalar", out=orig[:, :], in0=sc[:, :], scalar1=col2[:, :],
           scalar2=None, op0=ALU.mult)
        if gmode == "l3":
            vx("reduce_max", out=col1[:, :], in_=orig[:, :], axis=AX)
            vx("tensor_scalar", out=bcl[:, :], in0=zz[:, :], scalar1=col1[:, :],
               scalar2=None, op0=ALU.add)
            vx("tensor_tensor", out=oh1f[:, :], in0=orig[:, :], in1=bcl[:, :],
               op=ALU.is_equal)
            vx("scalar_tensor_tensor", out=t1[:, :], in0=oh1f[:, :],
               scalar=-BIGNEG, in1=orig[:, :], op0=ALU.mult, op1=ALU.add)
            vx("reduce_max", out=col2[:, :], in_=t1[:, :], axis=AX)
            vx("tensor_scalar", out=bcl[:, :], in0=zz[:, :], scalar1=col2[:, :],
               scalar2=None, op0=ALU.add)
            vx("tensor_tensor", out=oh2f[:, :], in0=t1[:, :], in1=bcl[:, :],
               op=ALU.is_equal)
            de = vx("tensor_tensor", out=col3[:, :], in0=col1[:, :],
                    in1=col2[:, :], op=ALU.subtract)
            P.wait(P.S, de)
            pe = P.S.activation(out=col3[:, :], in_=col3[:, :], func=AF.Sigmoid,
                                _inc=("ev", 1))
            P.wait(P.V, pe)
            vx("tensor_scalar", out=t1[:, :], in0=oh1f[:, :], scalar1=col3[:, :],
               scalar2=None, op0=ALU.mult)
            vx("tensor_scalar", out=col2[:, :], in0=col3[:, :], scalar1=-1.0,
               scalar2=1.0, op0=ALU.mult, op1=ALU.add)
            vx("tensor_scalar", out=oh2f[:, :], in0=oh2f[:, :], scalar1=col2[:, :],
               scalar2=None, op0=ALU.mult)
            ce = vx("tensor_tensor", out=comb[:, :], in0=t1[:, :],
                    in1=oh2f[:, :], op=ALU.add)
        else:
            vx("tensor_tensor", out=sc[:, :], in0=orig[:, :], in1=gb_bcast,
               op=ALU.add)
            g4 = sc[:, :].rearrange("p (g k) -> p g k", g=4)
            vx("tensor_reduce", out=gcol[:, :], in_=g4, axis=AX, op=ALU.max)
            vx("tensor_tensor",
               out=oh1f[:, :].rearrange("p (g k) -> p g k", g=4),
               in0=sc[:, :].rearrange("p (g k) -> p g k", g=4),
               in1=gcol[:, :].unsqueeze(2).broadcast_to([128, 4, 4]),
               op=ALU.is_equal)
            vx("scalar_tensor_tensor", out=t1[:, :], in0=oh1f[:, :],
               scalar=-BIGNEG, in1=sc[:, :], op0=ALU.mult, op1=ALU.add)
            vx("tensor_reduce", out=gcol2[:, :],
               in_=t1[:, :].rearrange("p (g k) -> p g k", g=4),
               axis=AX, op=ALU.max)
            vx("tensor_tensor", out=gcol[:, :], in0=gcol[:, :], in1=gcol2[:, :],
               op=ALU.add)
            vx("reduce_max", out=col1[:, :], in_=gcol[:, :], axis=AX)
            vx("tensor_scalar", out=bcl[:, 0:4], in0=zz[:, 0:4],
               scalar1=col1[:, :], scalar2=None, op0=ALU.add)
            vx("tensor_tensor", out=gcol2[:, :], in0=gcol[:, :], in1=bcl[:, 0:4],
               op=ALU.is_equal)
            vx("scalar_tensor_tensor", out=gcol2[:, :], in0=gcol2[:, :],
               scalar=-BIGNEG, in1=gcol[:, :], op0=ALU.mult, op1=ALU.add)
            vx("reduce_max", out=col2[:, :], in_=gcol2[:, :], axis=AX)
            vx("tensor_scalar", out=bcl[:, 0:4], in0=zz[:, 0:4],
               scalar1=col2[:, :], scalar2=None, op0=ALU.add)
            vx("tensor_tensor", out=gcol2[:, :], in0=gcol[:, :], in1=bcl[:, 0:4],
               op=ALU.is_ge)
            vx("tensor_scalar", out=gcol2[:, :], in0=gcol2[:, :], scalar1=BIGNEG,
               scalar2=-BIGNEG, op0=ALU.mult, op1=ALU.add)
            vx("tensor_tensor",
               out=t1[:, :].rearrange("p (g k) -> p g k", g=4),
               in0=sc[:, :].rearrange("p (g k) -> p g k", g=4),
               in1=gcol2[:, :].unsqueeze(2).broadcast_to([128, 4, 4]),
               op=ALU.add)
            vx("reduce_max", out=col1[:, :], in_=t1[:, :], axis=AX)
            vx("tensor_scalar", out=bcl[:, :], in0=zz[:, :], scalar1=col1[:, :],
               scalar2=None, op0=ALU.add)
            vx("tensor_tensor", out=oh1f[:, :], in0=t1[:, :], in1=bcl[:, :],
               op=ALU.is_equal)
            vx("tensor_tensor", out=sc[:, :], in0=orig[:, :], in1=oh1f[:, :],
               op=ALU.mult)
            vx("reduce_sum", out=col1[:, :], in_=sc[:, :], axis=AX)
            vx("scalar_tensor_tensor", out=t1[:, :], in0=oh1f[:, :],
               scalar=-BIGNEG, in1=t1[:, :], op0=ALU.mult, op1=ALU.add)
            vx("reduce_max", out=col2[:, :], in_=t1[:, :], axis=AX)
            vx("tensor_scalar", out=bcl[:, :], in0=zz[:, :], scalar1=col2[:, :],
               scalar2=None, op0=ALU.add)
            vx("tensor_tensor", out=oh2f[:, :], in0=t1[:, :], in1=bcl[:, :],
               op=ALU.is_equal)
            vx("tensor_tensor", out=sc[:, :], in0=orig[:, :], in1=oh2f[:, :],
               op=ALU.mult)
            vx("reduce_sum", out=col2[:, :], in_=sc[:, :], axis=AX)
            vx("tensor_scalar", out=oh1f[:, :], in0=oh1f[:, :], scalar1=col1[:, :],
               scalar2=None, op0=ALU.mult)
            vx("tensor_scalar", out=oh2f[:, :], in0=oh2f[:, :], scalar1=col2[:, :],
               scalar2=None, op0=ALU.mult)
            ce = vx("tensor_tensor", out=comb[:, :], in0=oh1f[:, :],
                    in1=oh2f[:, :], op=ALU.add)
        if (_os.environ.get("MOE_DBG") == "comb" and P._gdbg is not None
                and _os.environ.get("MOE_DBG2", "comb") != "logits"):
            dbgsel = {"comb": comb, "orig": orig, "sc": sc, "t1": t1,
                      "oh1f": oh1f, "oh2f": oh2f}[
                _os.environ.get("MOE_DBG2", "comb")]
            P.wait(P.S, P.ev("dv"))
            P.S.copy(out=P._gdbg[:, ti * nE:(ti + 1) * nE], in_=dbgsel[:, :],
                     _inc=("ev", 1))
        trb, trap, _ = psum.alloc(1, P.T)
        ce2 = P.ev("dv")
        P.wait(P.T, ce2, ctx.init_ev)
        tp = P.T.transpose(trap[:, 0:64].bitcast(BF16)[0:nE, 0:128],
                           comb[:, :], ctx.eye_b[:, :], _inc=("mm", 1))
        P.wait(P.S, tp)
        ge = P.S.copy(out=gstagE[0:nE, :],
                      in_=trap[:, 0:64].bitcast(BF16)[0:nE, 0:128],
                      _inc=("ev", 1))
        P.wait(P.T, ge)
        P.wait(P.Y, ge)
        for e in range(nE):
            de = P.dma("sync", grow_tiles[e][0:1, 128 * ti:128 * ti + 128],
                       gstagE[e:e + 1, :], sem="grw")
        P.wait(P.V, de)
        P.wait(P.S, de)
        psum.release(trb, P.ev("ev"))
        return P.ev("ev")

    e_proj_tok(P, wg_dram, xn_bf, D, nE, TOK, cb, psum, name=name + "w",
               nbw=max(nE, 128))
    return P.ev("ev")


def e_glu_h(P, psum, w_dram, x_bf_fn, hbuf, nhc, b_col_tile, func, grow=None,
            ones_bf=None, name="glu", scr1=None, scr2=None, hstage=None):
    wst = {"a_ev": None, "m_ev": None}
    if grow is not None:
        gb_, gbap, _ = psum.alloc(1, P.T)

    def h_cb(m0, mw, n0, nw, pstile, ev_mm):
        mb = m0 // 128
        hc = mb // 2
        if mb % 2 == 0:
            P.wait(P.V, ev_mm)
            e = P.V.tensor_scalar(out=scr1[:, n0:n0 + nw], in0=pstile[0:mw, 0:nw],
                                  scalar1=b_col_tile[:, mb:mb + 1], scalar2=None,
                                  op0=ALU.add, _inc=("dv", 1))
            wst["a_ev"] = e
            return e
        P.wait(P.S, ev_mm)
        if wst["m_ev"] is not None:
            P.wait(P.S, wst["m_ev"])
        e1 = P.S.activation(out=scr2[:, n0:n0 + nw], in_=pstile[0:mw, 0:nw],
                            func=func, bias=b_col_tile[:, mb:mb + 1], _inc=("ev", 1))
        P.wait(P.V, e1, wst["a_ev"])
        e2 = P.V.tensor_tensor(out=hstage[:, n0:n0 + nw], in0=scr1[:, n0:n0 + nw],
                               in1=scr2[:, n0:n0 + nw], op=ALU.mult, _inc=("dv", 1))
        if grow is not None:
            P.wait(P.T, e2)
            bce = P.T.matmul(gbap[:, 0:nw], ones_bf[0:1, 0:128],
                             grow[0:1, n0:n0 + nw], start=True, stop=True,
                             _inc=("mm", 1))
            P.wait(P.V, bce)
            e2 = P.V.tensor_tensor(out=hstage[:, n0:n0 + nw], in0=hstage[:, n0:n0 + nw],
                                   in1=gbap[:, 0:nw], op=ALU.mult, _inc=("dv", 1))
        wst["m_ev"] = e2
        P.wait(P.Y, e2)
        st = P.dma("sync", hbuf[128 * hc:128 * (hc + 1), n0:n0 + nw],
                   hstage[:, n0:n0 + nw], sem="fhb")
        P.wait(P.V, st)
        P.wait(P.S, st)
        return e2

    e_proj_fm(P, w_dram, x_bf_fn, D, nhc * 256, TOK, h_cb, psum, name=name)
    if grow is not None:
        psum.release(gb_, P.ev("dv"))
    return P.ev("wy")


def e_expert_out(P, psum, w2_dram, kin, hbuf, hstream, yrmw, b2_fold=None,
                 name="eo", sem_prefix="xhs"):
    nk = cdiv(kin, 128)
    ns = len(hstream)
    st = {"i": 0}
    h_done = P.ev("fhb")

    def xsrc(k, n0, nw):
        i = st["i"] % ns
        st["i"] += 1
        if st["i"] > ns:
            P.wait(P.G, P.ev("mm"))
        P.wait(P.G, h_done)
        ld = P.dma("gpsimd", hstream[i][:, 0:nw],
                   hbuf[128 * k:128 * k + 128, n0:n0 + nw],
                   sem=f"{sem_prefix}{i}")
        return hstream[i][:, 0:nw], ld

    def extra(m0, mw, n0, nw, pstile):
        b2row, grow = b2_fold
        return P.T.matmul(pstile[0:mw, 0:nw], b2row[0:1, m0:m0 + mw],
                          grow[0:1, n0:n0 + nw], start=False, stop=True,
                          _inc=("mm", 1))

    def cb(m0, mw, n0, nw, pstile, ev_mm):
        mb = m0 // 128
        P.wait(P.S, ev_mm)
        e = P.S.copy(out=P.moe_fst[0:mw, 0:nw], in_=pstile[0:mw, 0:nw],
                     _inc=("ev", 1))
        yrmw.add_psum(mb, n0, nw, P.moe_fst[0:mw, 0:nw], e)
        P.wait(P.S, P.ev("dv"))
        return P.ev("dv")

    ws = WeightStream(P, w2_dram, kin, D, name=name)
    pool = PsumPool(P, psum, 3)
    e_matmul_stack(P, ws, xsrc, TOK, cb,
                   pool, extra_mm=extra if b2_fold is not None else None)
    pool.free_all()


def build_moe_prog(l, W):
    P = Prog()
    xin = P.din("x", [D, TOK])
    xnin = P.din("xnb", [D, TOK], BF16)
    y = P.dout("y", [D, TOK])
    nE = 8 if l == 3 else 16
    xn = P.sb("xnt", [128, NKC * TOK], BF16)
    for k in range(NKC):
        xev = P.dma("gpsimd", xn[:, k * TOK:(k + 1) * TOK],
                    xnin[128 * k:128 * (k + 1), :], sem="cstg")
    cpev = P.dma("sync", y[:, :], xin[:, :], sem="cst")
    P.barrier_all([xev, cpev])
    psum = Psum(P)
    lnsp = NormScratch(P, psum)
    ctx = AttnCtx(P, psum, "mx")
    P.moe_fst = P.sb("moefst", [128, 512], F32)
    scr1 = P.sb("mscr1", [128, TOK], BF16)
    scr2 = P.sb("mscr2", [128, TOK], BF16)
    hstage = P.sb("mhst", [128, TOK], BF16)
    growt = P.sb("mgrow", [1, TOK], BF16)
    hstream = [P.sb(f"mhs{i}", [128, 512], BF16) for i in range(33)]
    growd = P.dram("growd", [nE, TOK], BF16)
    hbuf = P.dram("mhbuf", [32 * 128, TOK], BF16)
    yrmw = DramRMW(P, "my", y)

    wg = P.inp("wg", W["wg"])
    gbb = P.inp("gbb", W["gbb"])
    gbt = P.sb("gbt", [128, nE], F32)
    gbe = P.dma("sync", gbt[:, :], gbb[:, :], sem="cst")
    P.wait(P.V, gbe)
    grow_writer = {"last": None}
    if _os.environ.get("MOE_DBG") == "comb":
        P._gdbg = P.sb("gdbg_t", [128, 16 * nE], BF16)
        gdbg_d = P.dout("gdbg", [128, 16 * nE], BF16)

    gstage = P.sb("mgst", [1, 128], BF16)

    class GrowSink:
        def __getitem__(self, e):
            return _GrowRow(e)

    class _GrowRow:
        def __init__(self, e):
            self.e = e

        def __getitem__(self, idx):
            return growd[self.e:self.e + 1, idx[1]]

    gate_tiles = []

    class _RowTile:
        def __init__(self, e):
            self.e = e

        def __getitem__(self, idx):
            raise RuntimeError("handled in e_gate_rows2")

    growsb = P.sb("mgrowsb", [1, nE * TOK], BF16)
    grow_tiles = [_SubRow(growsb, e, TOK) for e in range(nE)]
    e_gate_rows(P, psum, ctx, xn, wg, nE, gbt[:, :], "l3" if l == 3 else "l2",
                grow_tiles, name="mg")
    gate_ev = P.ev("ev")
    P.barrier_all([gate_ev, P.ev("grw")])
    if _os.environ.get("MOE_DBG") == "comb":
        P.wait(P.G, P.ev("ev"))
        gde = P.dma("gpsimd", gdbg_d[:, :], P._gdbg[:, :], sem="xout")
        P.wait(P.G, gde)
        P.wait(P.Y, gde)
        nc = P.finalize()
        return P, nc
    if _os.environ.get("MOE_DBG") == "gate":
        for e in range(nE):
            ge_ = P.dma("gpsimd", y[e:e + 1, :], growsb[0:1, e * TOK:(e + 1) * TOK],
                        sem="xout")
        P.wait(P.G, ge_)
        P.wait(P.Y, ge_)
        nc = P.finalize()
        return P, nc

    if l == 3:
        b2rows = []
        w1l, b1l, w2l = [], [], []
        for e in range(nE):
            w1l.append(P.inp(f"w1_{e}", W["w1"][e]))
            b1l.append(P.inp(f"b1c_{e}", W["b1c"][e]))
            w2l.append(P.inp(f"w2_{e}", W["w2"][e]))
            b2d = P.inp(f"b2r_{e}", W["b2r"][e])
            b2t = P.sb(f"b2t_{e}", [1, D], BF16)
            P.dma("gpsimd", b2t[:, :], b2d[:, :], sem="cstg")
            b2rows.append(b2t)
        b1t = P.sb("b1t", [128, 32], F32)
        P.barrier_all([P.ev("cstg")])
        for e in range(nE):
            b1e = P.dma("sync", b1t[:, :], w1l and b1l[e][:, :], sem="cst")
            P.wait(P.S, b1e)
            gle = P.dma("sync", growt[:, :],
                        growsb[0:1, e * TOK:(e + 1) * TOK], sem="cst")
            P.wait(P.T, gle)
            gb_, gbap, _ = psum.alloc(1, P.T)

            def h_cb(m0, mw, n0, nw, pstile, ev_mm, e=e):
                mb = m0 // 128
                P.wait(P.S, ev_mm)
                eh = P.S.activation(out=scr1[:, n0:n0 + nw], in_=pstile[0:mw, 0:nw],
                                    func=AF.Relu, bias=b1t[:, mb:mb + 1],
                                    _inc=("ev", 1))
                P.wait(P.T, eh)
                bce = P.T.matmul(gbap[:, 0:nw], ctx.ones2k[0:1, 0:128],
                                 growt[0:1, n0:n0 + nw], start=True, stop=True,
                                 _inc=("mm", 1))
                P.wait(P.V, bce, eh)
                e2 = P.V.tensor_tensor(out=hstage[:, n0:n0 + nw],
                                       in0=scr1[:, n0:n0 + nw],
                                       in1=gbap[:, 0:nw], op=ALU.mult,
                                       _inc=("dv", 1))
                P.wait(P.Y, e2)
                st = P.dma("sync", hbuf[m0:m0 + mw, n0:n0 + nw],
                           hstage[:, n0:n0 + nw], sem="fhb")
                P.wait(P.V, st)
                P.wait(P.S, st)
                return e2

            e_proj_fm(P, w1l[e], xn, D, 4096, TOK, h_cb, psum, name=f"mw1_{e}")
            psum.release(gb_, P.ev("dv"))
            e_expert_out(P, psum, w2l[e], 4096, hbuf, hstream, yrmw,
                         b2_fold=(b2rows[e], growt), name=f"me{e}")
            P.wait(P.Y, P.ev("mm"))
            P.wait(P.Y, P.ev("ev"))
            P.wait(P.G, P.ev("mm"))
            P.wait(P.S, P.ev("mm"))
            P.wait(P.T, P.ev("dv"))
    else:
        wsi = P.inp("wsi", W["wsi"])
        bsi = P.inp("bsi", W["bsi"])
        bsit = P.sb("bsit", [128, 16], F32)
        P.dma("sync", bsit[:, :], bsi[:, :], sem="cst")
        bs2 = P.inp("bs2c", W["bs2c"])
        bs2t = P.sb("bs2t", [128, 8], F32)
        be2 = P.dma("sync", bs2t[:, :], bs2[:, :], sem="cst")
        P.barrier_all([be2])
        e_glu_h(P, psum, wsi, xn, hbuf, 8, bsit, AF.Silu, grow=None,
                name="msh", scr1=scr1, scr2=scr2, hstage=hstage)
        ws2 = P.inp("ws2", W["ws2"])
        onesrow = P.sb("monesr", [1, TOK], BF16)
        P.G.memset(onesrow[:, :], 1.0)
        bs2r = P.inp("bs2r", W["bs2r"])
        bs2rt = P.sb("bs2rt", [1, D], BF16)
        P.dma("gpsimd", bs2rt[:, :], bs2r[:, :], sem="cstg")
        P.barrier_all([P.ev("cstg")])
        e_expert_out(P, psum, ws2, 1024, hbuf, hstream, yrmw,
                     b2_fold=(bs2rt, onesrow), name="mso")
        P.wait(P.Y, P.ev("mm"))
        P.wait(P.G, P.ev("mm"))
        P.wait(P.S, P.ev("mm"))
        P.wait(P.T, P.ev("dv"))
        b13t = P.sb("b13t", [128, 8], F32)
        be2t = P.sb("be2t", [1, D], BF16)
        for e in range(nE):
            w13 = P.inp(f"w13_{e}", W["w13i"][e])
            b13 = P.inp(f"b13_{e}", W["b13c"][e])
            w2e = P.inp(f"we2_{e}", W["we2"][e])
            be2d = P.inp(f"be2r_{e}", W["be2r"][e])
            P.dma("gpsimd", be2t[:, :], be2d[:, :], sem="cstg")
            P.barrier_all([P.ev("cstg")])
            b13e = P.dma("sync", b13t[:, :], b13[:, :], sem="cst")
            gle = P.dma("sync", growt[:, :],
                        growsb[0:1, e * TOK:(e + 1) * TOK], sem="cst")
            P.wait(P.S, b13e)
            P.wait(P.T, gle)
            P.wait(P.V, gle)
            e_glu_h(P, psum, w13, xn, hbuf, 4, b13t, AF.Silu, grow=growt,
                    ones_bf=ctx.ones2k, name=f"m13_{e}",
                    scr1=scr1, scr2=scr2, hstage=hstage)
            e_expert_out(P, psum, w2e, 512, hbuf, hstream, yrmw,
                         b2_fold=(be2t, growt), name=f"mo{e}")
            P.wait(P.Y, P.ev("mm"))
            P.wait(P.Y, P.ev("ev"))
            P.wait(P.G, P.ev("mm"))
            P.wait(P.S, P.ev("mm"))
            P.wait(P.T, P.ev("dv"))
    P.wait(P.G, P.ev("wy"))
    for i in range(len(yrmw.sems)):
        if yrmw.guard[i] is not None:
            P.wait(P.G, yrmw.guard[i])
            P.wait(P.Y, yrmw.guard[i])
    nc = P.finalize()
    return P, nc


class _SubRow:

    def __init__(self, tile, e, tok):
        self.t = tile
        self.e = e
        self.tok = tok

    def __getitem__(self, idx):
        p, c = idx
        base = self.e * self.tok
        return self.t[p, base + (c.start or 0):base + c.stop]


def prep_moe_weights(l, inp):
    C = np.ascontiguousarray
    if l == 3:
        W1, b1 = inp["L3_ffn_W1"], inp["L3_ffn_b1"]
        W2, b2 = inp["L3_ffn_W2"], inp["L3_ffn_b2"]
        return dict(wg=C(inp["L3_ffn_Wg"]),
                    gbb=np.ascontiguousarray(
                        np.broadcast_to(inp["L3_ffn_bg"].reshape(1, 8),
                                        (128, 8)).copy()),
                    w1=[C(W1[e]) for e in range(8)],
                    b1c=[col_form(b1[e]) for e in range(8)],
                    w2=[C(W2[e]) for e in range(8)],
                    b2r=[C(b2[e].reshape(1, D)) for e in range(8)])
    E = 16
    We1, be1 = inp["L2_ffn_We1"], inp["L2_ffn_be1"]
    We2, be2 = inp["L2_ffn_We2"], inp["L2_ffn_be2"]
    We3, be3 = inp["L2_ffn_We3"], inp["L2_ffn_be3"]
    w13i, b13c = [], []
    for e in range(E):
        wi = np.zeros((D, 1024), np.float32)
        bc = np.zeros((128, 8), np.float32)
        for hc in range(4):
            wi[:, 256 * hc:256 * hc + 128] = We3[e][:, 128 * hc:128 * (hc + 1)]
            wi[:, 256 * hc + 128:256 * hc + 256] = We1[e][:, 128 * hc:128 * (hc + 1)]
            bc[:, 2 * hc] = be3[e][128 * hc:128 * (hc + 1)]
            bc[:, 2 * hc + 1] = be1[e][128 * hc:128 * (hc + 1)]
        w13i.append(wi)
        b13c.append(bc)
    ws1, bs1 = inp["L2_ffn_ws1"], inp["L2_ffn_bs1"]
    ws3, bs3 = inp["L2_ffn_ws3"], inp["L2_ffn_bs3"]
    wsi = np.zeros((D, 2048), np.float32)
    bsi = np.zeros((128, 16), np.float32)
    for hc in range(8):
        wsi[:, 256 * hc:256 * hc + 128] = ws3[:, 128 * hc:128 * (hc + 1)]
        wsi[:, 256 * hc + 128:256 * hc + 256] = ws1[:, 128 * hc:128 * (hc + 1)]
        bsi[:, 2 * hc] = bs3[128 * hc:128 * (hc + 1)]
        bsi[:, 2 * hc + 1] = bs1[128 * hc:128 * (hc + 1)]
    return dict(wg=C(inp["L2_ffn_gate_w"].T),
                gbb=np.ascontiguousarray(
                    np.broadcast_to(inp["L2_ffn_gate_b"].reshape(1, E),
                                    (128, E)).copy()),
                w13i=w13i, b13c=b13c,
                we2=[C(We2[e]) for e in range(E)],
                be2r=[C(be2[e].reshape(1, D)) for e in range(E)],
                wsi=wsi, bsi=bsi, ws2=C(inp["L2_ffn_ws2"]),
                bs2c=col_form(inp["L2_ffn_bs2"]),
                bs2r=C(inp["L2_ffn_bs2"].reshape(1, D)))


_LAST_RES = None


def run_layer_moe(l, W, x_shards, xn_shards):
    global _LAST_RES
    P, nc = get_prog(("moe", l), build_moe_prog, l, W)
    res = run_prog(P, nc, {"x": x_shards, "xnb": xn_shards})
    _LAST_RES = res
    return [r.get("y", r[list(r.keys())[0]]) for r in res]



def build_delta_prog(W):
    P = Prog()
    x = P.din("x", [D, TOK])
    y = P.dout("y", [D, TOK])
    xnout = P.dout("xn", [D, TOK], BF16)
    H = 8
    INTER = 2816
    wq = [P.inp(f"dwq{l}", W["wq"][l]) for l in range(2)]
    wk = [P.inp(f"dwk{l}", W["wk"][l]) for l in range(2)]
    wv = [P.inp(f"dwv{l}", W["wv"][l]) for l in range(2)]
    wb = [P.inp(f"dwb{l}", W["wb"][l]) for l in range(2)]
    wo = [P.inp(f"dwo{l}", W["wo"][l]) for l in range(2)]
    wgi = [P.inp(f"dwg{l}", W["wgi"][l]) for l in range(2)]
    wd = [P.inp(f"dwd{l}", W["wd"][l]) for l in range(2)]
    n1d = P.inp("dn1", W["n1c"])
    n2d = P.inp("dn2", W["n2c"])
    ond = P.inp("don", W["onc"])
    find = P.inp("dfin", W["finc"])
    xn = P.sb("xnt", [128, NKC * TOK], BF16)
    hb = P.sb("dhb", [128, NKC * TOK], BF16)
    qt = P.sb("dqt", [128, NKC * TOK], BF16)
    kt = P.sb("dkt", [128, NKC * TOK], BF16)
    ncol = P.sb("dncol", [128, 16 + 16 + 2 + 8], F32)
    n1c = ncol[:, 0:16]
    n2c = ncol[:, 16:32]
    onc = ncol[:, 32:34]
    finc = ncol[:, 34:42]
    ne1 = P.dma("sync", ncol[:, 0:16], n1d[:, :], sem="cst")
    ne2 = P.dma("sync", ncol[:, 16:32], n2d[:, :], sem="cst")
    ne3 = P.dma("sync", ncol[:, 32:34], ond[:, :], sem="cst")
    ne4 = P.dma("sync", ncol[:, 34:42], find[:, :], sem="cst")
    x_ev = P.dma("sync", y[:, :], x[:, :], sem="cst")
    P.barrier_all([P.ev("cst")])
    psum = Psum(P)
    lnsp = NormScratch(P, psum)
    ctx = AttnCtx(P, psum, "dx")
    twos = P.sb("dtwos", [1, 128], BF16)
    tw_ev = P.G.memset(twos[:, :], 2.0, _inc=("gp", 1))
    P.wait(P.T, tw_ev)
    dfst = P.sb("dfst", [128, 512], F32)
    P.moe_fst = dfst
    scrA = P.sb("dscrA", [128, 512], F32)
    ub = [P.sb(f"du{i}", [128, 512], F32) for i in range(4)]
    oacc = P.sb("doacc", [128, 512], F32)
    rs = P.sb("drs", [1, 512], F32)
    rnb = P.sb("drnb", [1, 512], BF16)
    rowb = [P.sb(f"drow{i}", [1, 512], BF16) for i in range(2)]
    brt = [P.sb(f"dbrt{i}", [1, 512], BF16) for i in range(2)]
    bbf = P.sb("dbbf", [128, H], F32)
    bb2 = P.sb("dbb2", [128, H], BF16)
    bstagE = P.sb("dbstE", [H, 128], BF16)
    hstream = [P.sb(f"dhs{i}", [128, 512], BF16) for i in range(11)]
    xd = P.dram("dxd", [D, TOK])
    bet_d = P.dram("dbet", [H, TOK], BF16)
    hbuf = P.dram("dhbuf", [22 * 128, TOK], BF16)
    xdrmw = DramRMW(P, "dxr", xd)
    yrmw = DramRMW(P, "dyr", y)
    xsrc = DramSrc(P, "dxs", y, n=2)
    xdsrc = DramSrc(P, "dds", xd, n=2)

    def wout(k, n0, nw, tmp):
        return P.V.tensor_copy(out=xn[:, k * TOK + n0:k * TOK + n0 + nw],
                               in_=tmp, _inc=("dv", 1))

    ln_ev = e_layernorm(P, lnsp, xsrc.get, TOK, wout)
    P.wait(P.G, ln_ev)
    for k in range(NKC):
        xne = P.dma("gpsimd", xnout[128 * k:128 * (k + 1), :],
                    xn[:, k * TOK:(k + 1) * TOK], sem="xout")
    P.barrier_all([xne, ln_ev])
    for k in range(NKC):
        for nb in range(4):
            n0 = nb * 512
            ue = P.S.copy(out=dfst[:, :], in_=xn[:, k * TOK + n0:k * TOK + n0 + 512],
                          _inc=("ev", 1))
            P.wait(P.Y, ue)
            se = P.dma("sync", xd[128 * k:128 * k + 128, n0:n0 + 512],
                       dfst[:, :], sem="dxu")
            P.wait(P.S, se)
    P.barrier_all([P.ev("dxu")])

    for l in range(2):
        def wrms(k, n0, nw, tmp, gc=n1c, li=l):
            return P.V.tensor_scalar(out=hb[:, k * TOK + n0:k * TOK + n0 + nw],
                                     in0=tmp, scalar1=gc[:, li * 8 + k:li * 8 + k + 1],
                                     scalar2=None, op0=ALU.mult, _inc=("dv", 1))

        ev = e_layernorm(P, lnsp, xdsrc.get, TOK, wrms, eps=1e-6, mode="rms")
        P.barrier_all([ev, P.ev("mm"), P.ev("ev"), P.ev("dv")])

        def mk_cb(dst):
            def cb(m0, mw, n0, nw, pstile, ev_mm):
                mb = m0 // 128
                P.wait(P.S, ev_mm)
                return P.S.activation(out=dst[:, mb * TOK + n0:mb * TOK + n0 + nw],
                                      in_=pstile[0:mw, 0:nw], func=AF.Silu,
                                      _inc=("ev", 1))
            return cb

        e_proj_fm(P, wq[l], hb, D, D, TOK, mk_cb(qt), psum, name=f"dpq{l}")
        P.barrier_all([P.ev("ev"), P.ev("mm")])
        e_proj_fm(P, wk[l], hb, D, D, TOK, mk_cb(kt), psum, name=f"dpk{l}")
        P.barrier_all([P.ev("ev"), P.ev("mm")])
        e_proj_fm(P, wv[l], hb, D, D, TOK, mk_cb(xn), psum, name=f"dpv{l}")
        P.barrier_all([P.ev("ev"), P.ev("mm")])
        vt = xn

        def bcb(ti, m0, mw, pstile, ev_mm, li=l):
            P.wait(P.S, ev_mm)
            sg = P.S.activation(out=bb2[:, :], in_=pstile[0:128, 0:H],
                                func=AF.Sigmoid, _inc=("ev", 1))
            trb, trap, _ = psum.alloc(1, P.T)
            P.wait(P.T, sg, ctx.init_ev)
            tp = P.T.transpose(trap[:, 0:64].bitcast(BF16)[0:H, 0:128],
                               bb2[:, :], ctx.eye_b[:, :], _inc=("mm", 1))
            P.wait(P.S, tp)
            ge = P.S.copy(out=bstagE[0:H, :],
                          in_=trap[:, 0:64].bitcast(BF16)[0:H, 0:128],
                          _inc=("ev", 1))
            P.wait(P.T, ge)
            P.wait(P.Y, ge)
            for h in range(H):
                de = P.dma("sync", bet_d[h:h + 1, 128 * ti:128 * ti + 128],
                           bstagE[h:h + 1, :], sem="dgr")
            P.wait(P.S, de)
            psum.release(trb, P.ev("ev"))
            return P.ev("ev")

        e_proj_tok(P, wb[l], hb, D, H, TOK, bcb, psum, name=f"dpb{l}",
                   nbw=128)
        P.barrier_all([P.ev("dgr"), P.ev("ev"), P.ev("mm")])

        prb_, prow, _ = psum.alloc(1, P.T)
        pcb_, pbc, _ = psum.alloc(1, P.T)
        last_T = None
        st = {"r": 0, "b": 0}

        def rowred(src_ap, prod_ev):
            i = st["r"] % 2
            st["r"] += 1
            P.wait(P.T, prod_ev)
            P.wait(P.T, P.ev("ev"))
            me = P.T.matmul(prow[0:1, 0:512], lnsp.ones_col[:, :], src_ap,
                            start=True, stop=True, _inc=("mm", 1))
            P.wait(P.S, me)
            P.wait(P.S, P.ev("mm"))
            ee = P.S.copy(out=rowb[i][:, :], in_=prow[0:1, 0:512],
                          _inc=("ev", 1))
            return rowb[i], ee

        def bcast(row_ap, row_ev, lhs=None):
            P.wait(P.T, row_ev)
            P.wait(P.T, P.ev("dv"))
            return P.T.matmul(pbc[:, 0:512],
                              (lhs if lhs is not None else ctx.ones2k)[0:1, 0:128],
                              row_ap[0:1, 0:512], start=True, stop=True,
                              _inc=("mm", 1))

        def vmul(out_ap, in0_ap, bc_ev, in1_ap=None, op=ALU.mult):
            P.wait(P.V, bc_ev)
            return P.V.tensor_tensor(out=out_ap, in0=in0_ap,
                                     in1=in1_ap if in1_ap is not None else pbc[:, 0:512],
                                     op=op, _inc=("dv", 1))

        def l2n_blocks(tile, h):
            for b in range(4):
                sl = tile[:, h * TOK + b * 512:h * TOK + (b + 1) * 512]
                pe = P.V.tensor_tensor(out=scrA[:, :], in0=sl, in1=sl,
                                       op=ALU.mult, _inc=("dv", 1))
                P.wait(P.T, pe)
                P.wait(P.T, P.ev("ev"))
                me = P.T.matmul(prow[0:1, 0:512], lnsp.ones_col[:, :],
                                scrA[:, :], start=True, stop=True,
                                _inc=("mm", 1))
                P.wait(P.S, me)
                e1 = P.S.copy(out=rs[:, :], in_=prow[0:1, 0:512], _inc=("ev", 1))
                P.wait(P.V, e1)
                e2 = P.V.tensor_scalar(out=rs[:, :], in0=rs[:, :], scalar1=1e-6,
                                       scalar2=None, op0=ALU.add, _inc=("dv", 1))
                P.wait(P.S, e2)
                e3 = P.S.activation(out=rs[:, :], in_=rs[:, :], func=AF.Sqrt,
                                    _inc=("ev", 1))
                P.wait(P.V, e3)
                e4 = P.V.reciprocal(out=rs[:, :], in_=rs[:, :], _inc=("dv", 1))
                P.wait(P.S, e4)
                e5 = P.S.copy(out=rnb[:, :], in_=rs[:, :], _inc=("ev", 1))
                P.wait(P.T, e5)
                P.wait(P.T, P.ev("dv"))
                be = P.T.matmul(pbc[:, 0:512], ctx.ones2k[0:1, 0:128],
                                rnb[0:1, 0:512], start=True, stop=True,
                                _inc=("mm", 1))
                P.wait(P.V, be)
                P.V.tensor_tensor(out=sl, in0=sl, in1=pbc[:, 0:512],
                                  op=ALU.mult, _inc=("dv", 1))

        def dot_row(ta, ba, tb, bb_, h):
            pe = P.V.tensor_tensor(out=scrA[:, :],
                                   in0=ta[:, h * TOK + ba * 512:h * TOK + ba * 512 + 512],
                                   in1=tb[:, h * TOK + bb_ * 512:h * TOK + bb_ * 512 + 512],
                                   op=ALU.mult, _inc=("dv", 1))
            return rowred(scrA[:, :], pe)

        def beta_bc(h, b):
            i = st["b"] % 2
            st["b"] += 1
            P.wait(P.Y, P.ev("mm"))
            ld = P.dma("sync", brt[i][:, :],
                       bet_d[h:h + 1, b * 512:(b + 1) * 512], sem=f"dbr{i}")
            return bcast(brt[i], ld, lhs=twos)

        for h in range(H):
            l2n_blocks(qt, h)
            l2n_blocks(kt, h)
            vsl = [vt[:, h * TOK + b * 512:h * TOK + (b + 1) * 512] for b in range(4)]
            be = beta_bc(h, 0)
            u_ev = [None] * 4
            u_ev[0] = vmul(ub[0][:, :], vsl[0], be)
            for i in range(1, 4):
                acc_ev = None
                for j in range(i):
                    rw, rev = dot_row(kt, j, kt, i, h)
                    be = bcast(rw, rev)
                    pe = vmul(scrA[:, :], ub[j][:, :], be)
                    if acc_ev is None:
                        P.wait(P.V, pe)
                        acc_ev = P.V.tensor_tensor(out=oacc[:, :], in0=vsl[i],
                                                   in1=scrA[:, :], op=ALU.subtract,
                                                   _inc=("dv", 1))
                    else:
                        P.wait(P.V, pe)
                        acc_ev = P.V.tensor_tensor(out=oacc[:, :], in0=oacc[:, :],
                                                   in1=scrA[:, :], op=ALU.subtract,
                                                   _inc=("dv", 1))
                be = beta_bc(h, i)
                u_ev[i] = vmul(ub[i][:, :], oacc[:, :], be)
            for i in range(4):
                o_ev = None
                for j in range(i + 1):
                    rw, rev = dot_row(kt, j, qt, i, h)
                    be = bcast(rw, rev)
                    if o_ev is None:
                        o_ev = vmul(oacc[:, :], ub[j][:, :], be)
                    else:
                        pe = vmul(scrA[:, :], ub[j][:, :], be)
                        P.wait(P.V, pe)
                        o_ev = P.V.tensor_tensor(out=oacc[:, :], in0=oacc[:, :],
                                                 in1=scrA[:, :], op=ALU.add,
                                                 _inc=("dv", 1))
                pe = P.V.tensor_tensor(out=scrA[:, :], in0=oacc[:, :],
                                       in1=oacc[:, :], op=ALU.mult, _inc=("dv", 1))
                P.wait(P.T, pe)
                P.wait(P.T, P.ev("ev"))
                me = P.T.matmul(prow[0:1, 0:512], lnsp.ones_col[:, :],
                                scrA[:, :], start=True, stop=True, _inc=("mm", 1))
                P.wait(P.S, me)
                e1 = P.S.mul(out=rs[:, :], in_=prow[0:1, 0:512], mul=1.0 / 128,
                             _inc=("ev", 1))
                P.wait(P.V, e1)
                e2 = P.V.tensor_scalar(out=rs[:, :], in0=rs[:, :], scalar1=1e-6,
                                       scalar2=None, op0=ALU.add, _inc=("dv", 1))
                P.wait(P.S, e2)
                e3 = P.S.activation(out=rs[:, :], in_=rs[:, :], func=AF.Sqrt,
                                    _inc=("ev", 1))
                P.wait(P.V, e3)
                e4 = P.V.reciprocal(out=rs[:, :], in_=rs[:, :], _inc=("dv", 1))
                P.wait(P.S, e4)
                e5 = P.S.copy(out=rnb[:, :], in_=rs[:, :], _inc=("ev", 1))
                P.wait(P.T, e5)
                P.wait(P.T, P.ev("dv"))
                be = P.T.matmul(pbc[:, 0:512], ctx.ones2k[0:1, 0:128],
                                rnb[0:1, 0:512], start=True, stop=True,
                                _inc=("mm", 1))
                P.wait(P.V, be)
                ve = P.V.tensor_tensor(out=oacc[:, :], in0=oacc[:, :],
                                       in1=pbc[:, 0:512], op=ALU.mult,
                                       _inc=("dv", 1))
                P.wait(P.V, ve)
                ge = P.V.tensor_scalar(out=oacc[:, :], in0=oacc[:, :],
                                       scalar1=onc[:, l:l + 1], scalar2=None,
                                       op0=ALU.mult, _inc=("dv", 1))
                P.wait(P.S, ge)
                se = P.S.copy(out=hb[:, h * TOK + i * 512:h * TOK + (i + 1) * 512],
                              in_=oacc[:, :], _inc=("ev", 1))
                P.wait(P.V, se)
        psum.release(prb_, P.ev("ev"))
        psum.release(pcb_, P.ev("dv"))
        P.barrier_all([P.ev("ev"), P.ev("dv"), P.ev("mm")])

        def ocb(m0, mw, n0, nw, pstile, ev_mm):
            mb = m0 // 128
            P.wait(P.S, ev_mm)
            e = P.S.copy(out=dfst[0:mw, 0:nw], in_=pstile[0:mw, 0:nw],
                         _inc=("ev", 1))
            xdrmw.add_psum(mb, n0, nw, dfst[0:mw, 0:nw], e)
            P.wait(P.S, P.ev("dv"))
            return P.ev("dv")

        e_proj_fm(P, wo[l], hb, D, D, TOK, ocb, psum, name=f"dpo{l}")
        P.barrier_all([P.ev("ev"), P.ev("dv"), P.ev("mm"),
                       P.ev(xdrmw.sems[0]), P.ev(xdrmw.sems[1])])

        def wrms2(k, n0, nw, tmp, li=l):
            return P.V.tensor_scalar(out=hb[:, k * TOK + n0:k * TOK + n0 + nw],
                                     in0=tmp, scalar1=n2c[:, li * 8 + k:li * 8 + k + 1],
                                     scalar2=None, op0=ALU.mult, _inc=("dv", 1))

        ev = e_layernorm(P, lnsp, xdsrc.get, TOK, wrms2, eps=1e-6, mode="rms")
        P.barrier_all([ev, P.ev("mm"), P.ev("ev"), P.ev("dv")])
        zcol = P.sb(f"dzc{l}", [128, 44], F32)
        ze = P.G.memset(zcol[:, :], 0.0, _inc=("gp", 1))
        P.wait(P.S, ze)
        e_glu_h(P, psum, wgi[l], hb, hbuf, 22, zcol, AF.Silu, grow=None,
                name=f"dglu{l}", scr1=qt[:, 0:TOK], scr2=qt[:, TOK:2 * TOK],
                hstage=kt[:, 0:TOK])
        P.barrier_all([P.ev("ev"), P.ev("dv"), P.ev("mm"), P.ev("fhb")])
        e_expert_out(P, psum, wd[l], INTER, hbuf, hstream, xdrmw,
                     b2_fold=None, name=f"dwd{l}", sem_prefix="dhs")
        P.barrier_all([P.ev("ev"), P.ev("dv"), P.ev("mm"),
                       P.ev(xdrmw.sems[0]), P.ev(xdrmw.sems[1])])

    def wfin(k, n0, nw, tmp):
        e = P.V.tensor_scalar(out=dfst[:, 0:nw], in0=tmp,
                              scalar1=finc[:, k:k + 1], scalar2=None,
                              op0=ALU.mult, _inc=("dv", 1))
        yrmw.add_psum(k, n0, nw, dfst[:, 0:nw], e)
        P.wait(P.V, P.ev(yrmw.sems[(yrmw.i - 1) % 2]))
        return P.ev("dv")

    ev = e_layernorm(P, lnsp, xdsrc.get, TOK, wfin, eps=1e-6, mode="rms")
    P.barrier_all([ev, P.ev(yrmw.sems[0]), P.ev(yrmw.sems[1])])
    nc = P.finalize()
    return P, nc


def prep_delta_weights(inp):
    C = np.ascontiguousarray
    H = 8
    Wg = inp["L2_attn_Wg"]
    wgi = []
    for l in range(2):
        w = np.zeros((D, 5632), np.float32)
        for c in range(22):
            w[:, 256 * c:256 * c + 128] = Wg[l][:, 2816 + 128 * c:2816 + 128 * (c + 1)]
            w[:, 256 * c + 128:256 * c + 256] = Wg[l][:, 128 * c:128 * (c + 1)]
        wgi.append(w)
    n1c = np.zeros((128, 16), np.float32)
    n2c = np.zeros((128, 16), np.float32)
    for l in range(2):
        for k in range(8):
            n1c[:, l * 8 + k] = inp["L2_attn_norm1_w"][l][128 * k:128 * (k + 1)]
            n2c[:, l * 8 + k] = inp["L2_attn_norm2_w"][l][128 * k:128 * (k + 1)]
    onc = C(inp["L2_attn_onorm_w"].T)
    finc = col_form(inp["L2_attn_final_norm_w"])
    return dict(wq=[C(inp["L2_attn_Wq"][l]) for l in range(2)],
                wk=[C(inp["L2_attn_Wk"][l]) for l in range(2)],
                wv=[C(inp["L2_attn_Wv"][l]) for l in range(2)],
                wb=[C(inp["L2_attn_Wb"][l]) for l in range(2)],
                wo=[C(inp["L2_attn_Wo"][l]) for l in range(2)],
                wgi=wgi, wd=[C(inp["L2_attn_Wd"][l]) for l in range(2)],
                n1c=n1c, n2c=n2c, onc=onc, finc=finc)


def run_layer_delta(W, x_shards):
    P, nc = get_prog(("delta",), build_delta_prog, W)
    res = run_prog(P, nc, {"x": x_shards})
    return ([r["y"] for r in res], [r["xn"] for r in res])



def kernel(**inp):
    x = inp["L0_attn_x"]
    xsh = shard_fm(np.asarray(x, np.float32))
    ysh, xnsh = run_layer_attn(0, prep_attn_weights(0, inp), xsh)
    f0 = ffn_host_cfg_arrays(inp["L0_ffn_wqkv"], inp["L0_ffn_q_scale"],
                             inp["L0_ffn_k_scale"], inp["L0_ffn_wo"],
                             inp["L0_ffn_ln_g"], inp["L0_ffn_ln_b"],
                             inp["L0_ffn_w1"], inp["L0_ffn_b1"],
                             inp["L0_ffn_w2"], inp["L0_ffn_b2"])
    ysh = run_layer_ffn(f0, ysh, xnsh)
    ysh, xnsh = run_layer_attn(1, prep_attn_weights(1, inp), ysh)
    f1 = ffn_host_cfg_arrays(inp["L1_ffn_wqkv"], inp["L1_ffn_q_scale"],
                             inp["L1_ffn_k_scale"], inp["L1_ffn_wo"],
                             inp["L1_ffn_ln_g"], inp["L1_ffn_ln_b"],
                             inp["L1_ffn_w1"], inp["L1_ffn_b1"],
                             inp["L1_ffn_w2"], inp["L1_ffn_b2"])
    ysh = run_layer_ffn(f1, ysh, xnsh)
    ysh, xnsh = run_layer_delta(prep_delta_weights(inp), ysh)
    ysh = run_layer_moe(2, prep_moe_weights(2, inp), ysh, xnsh)
    ysh, xnsh = run_layer_attn(3, prep_attn_weights(3, inp), ysh)
    ysh = run_layer_moe(3, prep_moe_weights(3, inp), ysh, xnsh)
    out = unshard_fm(ysh).astype(np.float32)
    return out
